# version 08
# baseline (speedup 1.0000x reference)
import contextlib
import numpy as np
import ml_dtypes
import bass_rust
import concourse.bass as bass
import concourse.mybir as mybir
from concourse.engine_type import EngineType
from concourse.bass_utils import run_bass_kernel_spmd
from concourse.library_config import proxy as proxy_lib

F32 = mybir.dt.float32
BF16 = mybir.dt.bfloat16
I32 = mybir.dt.int32
AF = mybir.ActivationFunctionType
ALU = mybir.AluOpType
AX = mybir.AxisListType
ts = bass.ts

D = 1024
T = 2048
IT = 4096
NC_ = 8
NTC = 4
TCW = 512
NP = 8
NKT = 32
NQC = 4
QCW = 512
EPS = 1e-5


def _bucket_np(e):
    n = np.asarray(e, np.int64)
    is_small = n < 16
    nf = np.maximum(n, 1).astype(np.float32)
    val = (np.log(nf / np.float32(16.0)).astype(np.float32)
           / np.float32(np.log(np.float32(8.0))) * np.float32(16.0))
    vl = 16 + val.astype(np.int32)
    vl = np.minimum(vl, 31)
    ret = np.where(is_small, n, vl)
    return np.maximum(ret, 0).astype(np.int64)


def host_constants():
    c = {}
    tt = np.arange(368)
    c["OH"] = (_bucket_np(239 - tt)[None, :] == np.arange(32)[:, None]).astype(ml_dtypes.bfloat16)
    tt2 = np.arange(256)
    c["OHP"] = (_bucket_np(255 - tt2)[None, :] == np.arange(32)[:, None]).astype(ml_dtypes.bfloat16)
    c["ident"] = np.eye(128, dtype=np.float32)
    bd = np.zeros((8, 128, 16), dtype=ml_dtypes.bfloat16)
    for p in range(8):
        bd[p, 0:64, 2 * p] = 1
        bd[p, 64:128, 2 * p + 1] = 1
    c["bd"] = bd
    return c


class Sem:
    def __init__(self, es, nc, name):
        self.h = es.enter_context(nc.semaphore(name))
        self.n = 0

    def inc(self, instr, k=1):
        instr.then_inc(self.h, k)
        self.n += k
        return instr


def custom_ap(src, pairs, offset):
    a = src
    a.ap = bass_rust.VecI64Pair(pairs)
    a.offset = offset
    return a


def own_regions(j, qc):
    a0, a1 = qc * QCW, qc * QCW + QCW
    wlo, whi = 128 * j, min(128 * j + 240, T)
    out = []
    hi0 = min(a1, whi)
    if hi0 > a0:
        blo, bhi = max(a0, wlo), min(hi0, whi)
        out.append((a0, hi0, 0, blo, max(bhi, blo)))
    lo1 = max(a0, whi)
    if a1 > lo1:
        out.append((lo1, a1, 1, 0, 0))
    return out


def partner_regions(j, qc):
    a0, a1 = qc * QCW, qc * QCW + QCW
    if j == 31 and qc == 0:
        return [(0, 128, 0, 0, 128), (128, QCW, 0, 0, 0)]
    return [(a0, a1, 0, 0, 0)]


def build(stage="full"):
    nc = bass.Bass("TRN2", target_bir_lowering=False)
    es = contextlib.ExitStack()

    dx = nc.dram_tensor("x", [T, D], F32, kind="ExternalInput").ap()
    dwqT = nc.dram_tensor("wqT", [D, D], F32, kind="ExternalInput").ap()
    dwkT = nc.dram_tensor("wkT", [D, D], F32, kind="ExternalInput").ap()
    dwvT = nc.dram_tensor("wvT", [D, D], F32, kind="ExternalInput").ap()
    dwoT = nc.dram_tensor("woT", [D, D], F32, kind="ExternalInput").ap()
    dW1 = nc.dram_tensor("W1", [D, IT], F32, kind="ExternalInput").ap()
    dW2 = nc.dram_tensor("W2", [IT, D], F32, kind="ExternalInput").ap()
    dbq = nc.dram_tensor("bq", [1, D], F32, kind="ExternalInput").ap()
    dbk = nc.dram_tensor("bk", [1, D], F32, kind="ExternalInput").ap()
    dbv = nc.dram_tensor("bv", [1, D], F32, kind="ExternalInput").ap()
    dbo = nc.dram_tensor("bo", [1, D], F32, kind="ExternalInput").ap()
    db1 = nc.dram_tensor("b1", [1, IT], F32, kind="ExternalInput").ap()
    db2 = nc.dram_tensor("b2", [1, D], F32, kind="ExternalInput").ap()
    drelb = nc.dram_tensor("relb", [32, 16], F32, kind="ExternalInput").ap()
    drelb0 = nc.dram_tensor("relb0", [32, 16], F32, kind="ExternalInput").ap()
    drelbT = nc.dram_tensor("relbT", [16, 32], F32, kind="ExternalInput").ap()
    dOH = nc.dram_tensor("OH", [32, 368], BF16, kind="ExternalInput").ap()
    dOHP = nc.dram_tensor("OHP", [32, 256], BF16, kind="ExternalInput").ap()
    dident = nc.dram_tensor("ident", [128, 128], F32, kind="ExternalInput").ap()
    dbd = nc.dram_tensor("bd", [8, 128, 16], BF16, kind="ExternalInput").ap()
    dpar2 = nc.dram_tensor("par2", [1, 2], I32, kind="ExternalInput").ap()
    dparf = nc.dram_tensor("parf", [32, 1], F32, kind="ExternalInput").ap()
    dy = nc.dram_tensor("y", [T, D], F32, kind="ExternalOutput").ap()

    w1bf = nc.dram_tensor("w1bf", [32, 128, 8, 128], BF16, kind="Internal").ap()
    w2bf = nc.dram_tensor("w2bf", [8, 128, 32, 128], BF16, kind="Internal").ap()
    wqbf = nc.dram_tensor("wqbf", [128, 8, 1024], BF16, kind="Internal").ap()
    wkbf = nc.dram_tensor("wkbf", [128, 8, 1024], BF16, kind="Internal").ap()
    wvbf = nc.dram_tensor("wvbf", [128, 8, 1024], BF16, kind="Internal").ap()
    wobf = nc.dram_tensor("wobf", [8, 128, 1024], BF16, kind="Internal").ap()
    qdram = nc.dram_tensor("qdram", [NP, 128, T], BF16, kind="Internal").ap()
    xdram = nc.dram_tensor("xdram", [NC_, 128, T], F32, kind="Internal").ap()
    frevd = nc.dram_tensor("frevd", [16, 368], F32, kind="Internal").ap()
    frevpd = nc.dram_tensor("frevpd", [16, 256], F32, kind="Internal").ap()
    kstage = nc.dram_tensor("kstage", [NP, 128, T], BF16, kind="Internal").ap()
    vstage = nc.dram_tensor("vstage", [NP, 128, 16, 130], BF16, kind="Internal").ap()
    kpar = nc.dram_tensor("kpar", [NP, 128, T], BF16, kind="Internal").ap()
    vpar = nc.dram_tensor("vpar", [NP, 128, 16, 130], BF16, kind="Internal").ap()
    kshk = nc.dram_tensor("kshk", [2, NP, 128, T], BF16, kind="Internal", addr_space="Shared").ap()
    kshv = nc.dram_tensor("kshv", [2, NP, 128, 16, 130], BF16, kind="Internal", addr_space="Shared").ap()
    kshm = nc.dram_tensor("kshm", [2, 16, 1], F32, kind="Internal", addr_space="Shared").ap()

    sT = Sem(es, nc, "sT")
    sA = Sem(es, nc, "sA")
    sD = Sem(es, nc, "sD")
    sG = Sem(es, nc, "sG")
    sSig = Sem(es, nc, "sSig")
    sPrep = Sem(es, nc, "sPrep")
    sPrepL = Sem(es, nc, "sPrepL")
    cvq = Sem(es, nc, "cvq")
    scq = Sem(es, nc, "scq")
    xq = [Sem(es, nc, f"xq{i}") for i in range(2)]
    spq = Sem(es, nc, "spq")
    ftq = Sem(es, nc, "ftq")
    bndq = Sem(es, nc, "bndq")
    shq1 = Sem(es, nc, "shq1")
    shq2 = Sem(es, nc, "shq2")
    pwq = Sem(es, nc, "pwq")
    evq_q = [Sem(es, nc, f"evq{i}") for i in range(4)]
    kvqA = [Sem(es, nc, f"kvqA{i}") for i in range(2)]
    kvqB = [Sem(es, nc, f"kvqB{i}") for i in range(2)]
    xcq = Sem(es, nc, "xcq")
    woq = [Sem(es, nc, f"woq{i}") for i in range(2)]
    ccq = Sem(es, nc, "ccq")
    vfq = Sem(es, nc, "vfq")
    stq = [Sem(es, nc, f"stq{i}") for i in range(2)]
    w1q = [Sem(es, nc, f"w1q{i}") for i in range(2)]
    w2q = [Sem(es, nc, f"w2q{i}") for i in range(2)]
    vsl = [Sem(es, nc, f"vsl{i}") for i in range(2)]
    vss = [Sem(es, nc, f"vss{i}") for i in range(2)]
    vrq = Sem(es, nc, "vrq")

    sbt = lambda name, shape, dt: es.enter_context(nc.sbuf_tensor(name, shape, dt))
    ident = sbt("identS", [128, 128], F32)
    onescol = sbt("onescol", [128, 1], BF16)
    onesf = sbt("onesf", [128, 128], F32)
    onesrow = sbt("onesrow", [1, 512], BF16)
    bqr = sbt("bqr", [1, D], BF16)
    bkr = sbt("bkr", [1, D], BF16)
    bvr = sbt("bvr", [1, D], BF16)
    bor = sbt("bor", [1, D], BF16)
    b1r = sbt("b1r", [1, IT], BF16)
    b2r = sbt("b2r", [1, D], BF16)
    bdsb = sbt("bdsb", [128, 8, 16], BF16)
    ps2 = sbt("ps2", [1, 2], I32)
    bandDr = sbt("bandDr", [128, 16, 240], F32)
    bndDr = sbt("bndDr", [128, 16, 128], F32)
    bcol = sbt("bcol", [128, 192], F32)
    qmax2 = sbt("qmax2", [128, 4], F32)
    kmaxt = sbt("kmaxt", [128, 6], F32)
    kmax2b = sbt("kmax2b", [128, 1], F32)
    lnrow = sbt("lnrow", [1, 8, 512], F32)
    rstdb = sbt("rstdb", [128, 512], BF16)
    mrsb = sbt("mrsb", [128, 512], BF16)
    relsb = sbt("relsb", [32, 64], F32)
    relbpb = sbt("relbpb", [32, 32], BF16)
    relT = sbt("relT", [16, 40], F32)
    Bt = sbt("Bt", [128, 16], F32)
    bcpm = sbt("bcpm", [16, 12], F32)
    bcrow = sbt("bcrow", [1, 192], F32)
    parfsb = sbt("parfsb", [32, 1], F32)
    vscrow = sbt("vscrow", [1, 16], F32)
    vsc16 = sbt("vsc16", [128, 16], F32)

    pa = es.enter_context(nc.psum_tensor("pa", [128, 8, 512], F32))

    def bank(i):
        return pa[:, i, :]

    blk_cm = nc.Block()
    blk = blk_cm.__enter__()
    TN, VE, SC, GP, SY = nc.tensor, nc.vector, nc.scalar, nc.gpsimd, nc.sync

    GP.load_library(proxy_lib)

    def emit_passthru():
        esmin = contextlib.ExitStack()
        xs0 = esmin.enter_context(nc.sbuf_tensor("xs0", [128, 2, D], F32))
        for tt in range(16):
            sl = tt % 2
            if tt >= 2:
                SY.wait_ge(stq[sl].h, 16 * (tt // 2))
            xq[sl].inc(SY.dma_start(out=xs0[:, sl, :], in_=dx[ts(tt, 128), :]), 16)
            SY.wait_ge(xq[sl].h, xq[sl].n)
            stq[sl].inc(SY.dma_start(out=dy[ts(tt, 128), :], in_=xs0[:, sl, :]), 16)
        SY.wait_ge(stq[0].h, 16 * 8)
        SY.wait_ge(stq[1].h, 16 * 8)
        blk_cm.__exit__(None, None, None)
        esmin.close()
        es.close()
        return nc

    if stage == "pt":
        return emit_passthru()
    for ii in range(32):
        cvq.inc(GP.dma_start(out=w1bf[ii, :, :, :],
                             in_=custom_ap(dW1[:, :], [(IT, 128), (128 * IT, 8), (1, 128)],
                                           ii * 128)), 16)
    for dd in range(8):
        for i4 in range(4):
            cvq.inc(GP.dma_start(out=w2bf[dd, :, 8 * i4:8 * i4 + 8, :],
                                 in_=custom_ap(dW2[:, :], [(D, 128), (128 * D, 8), (1, 128)],
                                               dd * 128 + 8 * i4 * 128 * D)), 16)
    for (wdst, wsrc) in ((wqbf, dwqT), (wkbf, dwkT), (wvbf, dwvT)):
        cvq.inc(GP.dma_start(out=wdst[:, :, :],
                             in_=custom_ap(wsrc[:, :], [(D, 128), (128 * D, 8), (1, 1024)], 0)), 16)
    cvq.inc(GP.dma_start(out=wobf[:, :, :], in_=dwoT.rearrange("(c p) d -> c p d", p=128)), 16)
    for (dst, src) in ((bqr, dbq), (bkr, dbk), (bvr, dbv), (bor, dbo), (b1r, db1), (b2r, db2)):
        cvq.inc(GP.dma_start(out=dst[:, :], in_=src[:, :]), 16)
    cvq.inc(GP.dma_start(out=bdsb[:, :, :], in_=dbd.rearrange("p r c -> r p c")), 16)
    cvq.inc(GP.dma_start(out=relsb[:, 0:16], in_=drelb[:, :]), 16)
    cvq.inc(GP.dma_start(out=relsb[:, 16:32], in_=drelb0[:, :]), 16)
    cvq.inc(GP.dma_start(out=relT[:, 0:32], in_=drelbT[:, :]), 16)
    m_cvq = cvq.n

    scq.inc(SY.dma_start(out=ident[:, :], in_=dident[:, :]), 16)
    scq.inc(SY.dma_start(out=ps2[:, :], in_=dpar2[:, :]), 16)
    scq.inc(SY.dma_start(out=parfsb[:, :], in_=dparf[:, :]), 16)
    m_scq = scq.n

    sG.inc(GP.memset(onescol[:, :], 1.0))
    sG.inc(GP.memset(onesf[:, :], 1.0))
    sG.inc(GP.memset(onesrow[:, :], 1.0))
    m_memset = sG.n
    esOB = contextlib.ExitStack()
    onesbig = esOB.enter_context(nc.sbuf_tensor("onesbig", [128, 2080], BF16))
    sG.inc(GP.memset(onesbig[:, :], 1.0))
    GP.wait_ge(sG.h, sG.n)
    for p in range(NP):
        vfq.inc(GP.dma_start(out=vstage[p, :, :, :], in_=onesbig[:, :].rearrange("p (a b) -> p a b", a=16)), 16)
    m_vfill = vfq.n
    esOB.close()

    if stage == "ptc":
        GP.wait_ge(cvq.h, m_cvq)
        return emit_passthru()

    esX = contextlib.ExitStack()
    X = esX.enter_context(nc.sbuf_tensor("X", [128, NC_, T], F32))
    esXs = contextlib.ExitStack()
    xstage = esXs.enter_context(nc.sbuf_tensor("xstage", [128, 2, D], F32))

    TN.wait_ge(scq.h, m_scq)
    VE.wait_ge(sG.h, m_memset)
    VE.wait_ge(vfq.h, m_vfill)
    SY.wait_ge(vfq.h, m_vfill)
    xtp_marks = []
    xev_marks = []
    for tt in range(16):
        sl = tt % 2
        if tt >= 2:
            SY.wait_ge(sT.h, xtp_marks[tt - 2])
        xq[sl].inc(SY.dma_start(out=xstage[:, sl, :], in_=dx[ts(tt, 128), :]), 16)
        TN.wait_ge(xq[sl].h, xq[sl].n)
        for cc in range(NC_):
            step = tt * NC_ + cc
            b = step % 4
            if step >= 4:
                TN.wait_ge(sD.h, xev_marks[step - 4])
            sT.inc(TN.transpose(bank(b)[:, 0:128], xstage[:, sl, ts(cc, 128)], ident[:, :]))
            VE.wait_ge(sT.h, sT.n)
            sD.inc(VE.tensor_copy(X[:, cc, ts(tt, 128)], bank(b)[:, 0:128]))
            xev_marks.append(sD.n)
        xtp_marks.append(sT.n)
    esXs.close()

    def emit_ln_pre(src_slices, Xb, sq):
        VE.drain()
        for c in range(NC_):
            sD.inc(VE.tensor_copy(Xb[:, c, :], src_slices[c]))
        VE.drain()
        for c in range(NC_):
            sD.inc(VE.tensor_tensor(sq[:, c, :], Xb[:, c, :], Xb[:, c, :], ALU.mult))
        return sD.n

    def emit_ln_stats(pre_mark, Xb, sq):
        TN.wait_ge(sD.h, pre_mark)
        for c in range(NC_):
            sT.inc(TN.matmul(bank(4)[0:1, :], onescol[:, 0:1], Xb[:, c, :],
                             start=(c == 0), stop=(c == NC_ - 1), tile_position=(0, 0),
                             skip_group_check=True))
        for c in range(NC_):
            sT.inc(TN.matmul(bank(5)[0:1, :], onescol[:, 0:1], sq[:, c, :],
                             start=(c == 0), stop=(c == NC_ - 1), tile_position=(0, 0),
                             skip_group_check=True))
        m_st = sT.n
        VE.wait_ge(sT.h, m_st)
        VE.tensor_scalar(lnrow[0:1, 0, :], bank(4)[0:1, :], 1.0 / D, None, ALU.mult)
        VE.tensor_scalar(lnrow[0:1, 1, :], bank(5)[0:1, :], 1.0 / D, None, ALU.mult)
        VE.drain()
        VE.tensor_tensor(lnrow[0:1, 2, :], lnrow[0:1, 0, :], lnrow[0:1, 0, :], ALU.mult)
        VE.drain()
        VE.tensor_tensor(lnrow[0:1, 3, :], lnrow[0:1, 1, :], lnrow[0:1, 2, :], ALU.subtract)
        VE.drain()
        sD.inc(VE.tensor_scalar(lnrow[0:1, 3, :], lnrow[0:1, 3, :], EPS, None, ALU.add))
        SC.wait_ge(sD.h, sD.n)
        SC.activation(lnrow[0:1, 6, :], lnrow[0:1, 3, :], AF.Ln)
        SC.drain()
        sA.inc(SC.activation(lnrow[0:1, 4, :], lnrow[0:1, 6, :], AF.Exp, scale=-0.5))
        VE.wait_ge(sA.h, sA.n)
        VE.drain()
        sD.inc(VE.tensor_tensor(lnrow[0:1, 5, :], lnrow[0:1, 0, :], lnrow[0:1, 4, :], ALU.mult))
        return sA.n

    def emit_ln_fin(Xb, sq, LNout):
        TN.wait_ge(sA.h, sA.n)
        sT.inc(TN.matmul(bank(6)[:, :], onesf[0:1, 0:128], lnrow[0:1, 4, :],
                         start=True, stop=True, tile_position=(0, 0)))
        VE.wait_ge(sT.h, sT.n)
        sD.inc(VE.tensor_copy(rstdb[:, :], bank(6)[:, :]))
        TN.wait_ge(sD.h, sD.n)
        sT.inc(TN.matmul(bank(6)[:, :], onesf[0:1, 0:128], lnrow[0:1, 5, :],
                         start=True, stop=True, tile_position=(0, 0)))
        VE.wait_ge(sT.h, sT.n)
        sD.inc(VE.tensor_copy(mrsb[:, :], bank(6)[:, :]))
        VE.drain()
        for c in range(NC_):
            VE.tensor_tensor(sq[:, c, :], Xb[:, c, :], rstdb[:, :], ALU.mult)
        VE.drain()
        for c in range(NC_):
            sD.inc(VE.tensor_tensor(LNout[:, c, :], sq[:, c, :], mrsb[:, :], ALU.subtract))
        return sD.n

    def emit_ln(src_slices, Xb, sq, LNout):
        pm = emit_ln_pre(src_slices, Xb, sq)
        emit_ln_stats(pm, Xb, sq)
        return emit_ln_fin(Xb, sq, LNout)

    fst = {"w1g": 0, "w2g": 0, "h": [0, 0], "hlast": 0,
           "w1cons": [0, 0], "w2cons": [0, 0], "ylast": [0, 0]}

    def w1_load(gi, w1buf):
        g = gi % 16 if False else None
        slq = w1q[gi % 2]
        GP.wait_ge(sT.h, fst["w1cons"][gi % 2])
        gg = gi - (gi // 16) * 16
        for iw in range(2):
            slq.inc(GP.dma_start(out=w1buf[:, gi % 2, iw, :, :],
                                 in_=w1bf[2 * gg + iw, :, :, :]), 16)

    def w2_load(di, w2buf):
        slq = w2q[di % 2]
        GP.wait_ge(sT.h, fst["w2cons"][di % 2])
        hs = di - (di // 16) * 16
        d, hf = hs // 2, hs % 2
        for iw in range(2):
            slq.inc(GP.dma_start(out=w2buf[:, di % 2, 8 * iw:8 * iw + 8, :],
                                 in_=w2bf[d, :, 16 * hf + 8 * iw:16 * hf + 8 * iw + 8, :]), 16)

    def emit_ffn(LNout, ln_mark, Xdst, w1buf, w2buf, hT, hooks={}):
        gbase = fst["w1g"]
        w1_load(gbase + 0, w1buf)
        w1_load(gbase + 1, w1buf)
        TN.wait_ge(sD.h, ln_mark)
        for i in range(32):
            gi = gbase + i // 2
            if i % 2 == 0:
                TN.wait_ge(w1q[gi % 2].h, 32 * (gi // 2 + 1))
            b = i % 2
            TN.wait_ge(sA.h, fst["h"][b])
            for c in range(NC_):
                TN.matmul(bank(b)[:, :], w1buf[:, gi % 2, i % 2, c, :], LNout[:, c, :],
                          start=(c == 0), stop=False, tile_position=(0, 0), skip_group_check=True)
            sT.inc(TN.matmul(bank(b)[:, :], b1r[0:1, ts(i, 128)], onesrow[0:1, :],
                             start=False, stop=True, tile_position=(0, 0), skip_group_check=True))
            if i % 2 == 1:
                fst["w1cons"][gi % 2] = sT.n
                if gi + 2 < gbase + 16:
                    w1_load(gi + 2, w1buf)
            SC.wait_ge(sT.h, sT.n)
            sA.inc(SC.activation(hT[:, i, :], bank(b)[:, :], AF.Relu))
            fst["h"][b] = sA.n
        fst["w1g"] += 16
        fst["hlast"] = sA.n
        dbase = fst["w2g"]
        w2_load(dbase + 0, w2buf)
        w2_load(dbase + 1, w2buf)
        ylast = fst["ylast"]
        for d in range(8):
            if d in hooks:
                hooks[d]()
            TN.wait_ge(sA.h, fst["hlast"])
            b = 2 + d % 2
            TN.wait_ge(sD.h, ylast[d % 2])
            for i in range(32):
                di = dbase + 2 * d + i // 16
                if i % 16 == 0:
                    TN.wait_ge(w2q[di % 2].h, 32 * (di // 2 + 1))
                TN.matmul(bank(b)[:, :], w2buf[:, di % 2, i % 16, :], hT[:, i, :],
                          start=(i == 0), stop=False, tile_position=(0, 0), skip_group_check=True)
                if i % 16 == 15:
                    fst["w2cons"][di % 2] = sT.n + 1
            sT.inc(TN.matmul(bank(b)[:, :], b2r[0:1, ts(d, 128)], onesrow[0:1, :],
                             start=False, stop=True, tile_position=(0, 0), skip_group_check=True))
            fst["w2cons"][(dbase + 2 * d) % 2] = sT.n
            fst["w2cons"][(dbase + 2 * d + 1) % 2] = sT.n
            for nx in (dbase + 2 * d + 2, dbase + 2 * d + 3):
                if nx < dbase + 16:
                    w2_load(nx, w2buf)
            VE.wait_ge(sT.h, sT.n)
            sD.inc(VE.scalar_tensor_tensor(Xdst[d], bank(b)[:, :], 0.5, Xdst[d], ALU.mult, ALU.add))
            ylast[d % 2] = sD.n
        fst["w2g"] += 16
        return sD.n

    st_state = {"n": 0, "marks": {}, "cnt": [0, 0]}

    def emit_store_chunk(Xsrc, tok0, stflat):
        TN.wait_ge(sD.h, sD.n)
        for tt in range(4):
            sl = tt % 2
            for cc in range(NC_):
                step = st_state["n"]
                st_state["n"] += 1
                b = step % 4
                if step >= 4:
                    TN.wait_ge(sD.h, st_state["marks"][step - 4])
                sT.inc(TN.transpose(bank(b)[:, 0:128], Xsrc[:, cc, ts(tt, 128)], ident[:, :]))
                VE.wait_ge(sT.h, sT.n)
                if cc == 0 and st_state["cnt"][sl] > 0:
                    VE.wait_ge(stq[sl].h, 16 * st_state["cnt"][sl])
                sD.inc(VE.tensor_copy(stflat[:, sl * 1024 + cc * 128: sl * 1024 + cc * 128 + 128],
                                      bank(b)[:, 0:128]))
                st_state["marks"][step] = sD.n
            SY.wait_ge(sD.h, sD.n)
            stq[sl].inc(SY.dma_start(out=dy[tok0 + tt * 128: tok0 + (tt + 1) * 128, :],
                                     in_=stflat[:, sl * 1024: sl * 1024 + 1024]), 16)
            st_state["cnt"][sl] += 1

    VE.wait_ge(cvq.h, m_cvq)
    VE.wait_ge(scq.h, m_scq)
    VE.tensor_tensor(relsb[:, 32:48], relsb[:, 0:16], relsb[:, 16:32], ALU.subtract)
    VE.drain()
    VE.tensor_scalar(relsb[:, 48:64], relsb[:, 32:48], parfsb[:, 0:1], None, ALU.mult)
    VE.drain()
    VE.tensor_copy(relbpb[:, 0:16], relsb[:, 32:48])
    sD.inc(VE.tensor_copy(relbpb[:, 16:32], relsb[:, 48:64]))
    VE.tensor_scalar(relT[:, 0:32], relT[:, 0:32], relT[:, 0:1], None, ALU.subtract)
    VE.drain()
    sD.inc(VE.reduce_max(relT[:, 32:33], relT[:, 0:32], axis=AX.X, apply_absolute_value=True))
    VE.drain()
    sD.inc(VE.tensor_scalar(relT[:, 34:35], relT[:, 31:32], parfsb[0:16, 0:1], None, ALU.mult))
    m_rel = sD.n

    esOH = contextlib.ExitStack()
    oh_s = esOH.enter_context(nc.sbuf_tensor("oh_s", [32, 368], BF16))
    ohp_s = esOH.enter_context(nc.sbuf_tensor("ohp_s", [32, 256], BF16))
    frow = esOH.enter_context(nc.sbuf_tensor("frow", [16, 368], F32))
    GP.wait_ge(sT.h, sT.n)
    GP.wait_ge(sD.h, sD.n)
    cvq.inc(GP.dma_start(out=oh_s[:, :], in_=dOHP[:, :] if False else dOH[:, :]), 16)
    cvq.inc(GP.dma_start(out=ohp_s[:, :], in_=dOHP[:, :]), 16)
    TN.wait_ge(cvq.h, cvq.n)
    TN.wait_ge(sD.h, sD.n)
    sT.inc(TN.matmul(bank(0)[0:16, 0:368], relbpb[:, 0:16], oh_s[:, :],
                     start=True, stop=True, tile_position=(0, 0)))
    VE.wait_ge(sT.h, sT.n)
    sD.inc(VE.tensor_copy(frow[:, :], bank(0)[0:16, 0:368]))
    SY.wait_ge(sD.h, sD.n)
    ftq.inc(SY.dma_start(out=frevd[:, :], in_=frow[:, :]), 16)
    TN.wait_ge(sD.h, sD.n)
    sT.inc(TN.matmul(bank(1)[0:16, 0:256], relbpb[:, 16:32], ohp_s[:, :],
                     start=True, stop=True, tile_position=(0, 0)))
    VE.wait_ge(sT.h, sT.n)
    VE.wait_ge(ftq.h, ftq.n)
    VE.tensor_copy(frow[:, 0:256], bank(1)[0:16, 0:256])
    VE.drain()
    sD.inc(VE.tensor_scalar(frow[:, 0:256], frow[:, 0:256], relT[:, 34:35], None, ALU.subtract))
    SY.wait_ge(sD.h, sD.n)
    ftq.inc(SY.dma_start(out=frevpd[:, :], in_=frow[:, 0:256]), 16)
    GP.wait_ge(ftq.h, ftq.n)
    for gh in range(16):
        bndq.inc(GP.dma_start(out=bandDr[:, gh, :],
                              in_=custom_ap(frevd[0:1, :], [(1, 128), (1, 240)], gh * 368)), 16)
        bndq.inc(GP.dma_start(out=bndDr[:, gh, :],
                              in_=custom_ap(frevpd[0:1, :], [(1, 128), (1, 128)], gh * 256)), 16)
    m_bndq = bndq.n
    esOH.close()
    VE.wait_ge(ftq.h, ftq.n)
    GP.wait_ge(ftq.h, ftq.n)

    esP1 = contextlib.ExitStack()
    Xb1 = esP1.enter_context(nc.sbuf_tensor("Xb1", [128, NC_, TCW], BF16))
    sq1 = esP1.enter_context(nc.sbuf_tensor("sq1", [128, NC_, TCW], BF16))
    LN1 = esP1.enter_context(nc.sbuf_tensor("LN1", [128, 2, NC_, TCW], BF16))
    w1buf = esP1.enter_context(nc.sbuf_tensor("w1buf", [128, 2, 2, 8, 128], BF16))
    w2buf = esP1.enter_context(nc.sbuf_tensor("w2buf", [128, 2, 16, 128], BF16))
    hT = esP1.enter_context(nc.sbuf_tensor("hT", [128, 32, TCW], BF16))

    TN.wait_ge(cvq.h, m_cvq)
    TN.wait_ge(sG.h, m_memset)
    VE.wait_ge(sG.h, m_memset)
    SC.wait_ge(sG.h, m_memset)
    GP.wait_ge(cvq.h, m_cvq)
    if stage != "xonly":
        srcs0 = [X[:, c, ts(0, TCW)] for c in range(NC_)]
        lnms = {0: emit_ln(srcs0, Xb1, sq1, LN1[:, 0, :, :])}
        for tc in range(NTC):
            srcs = [X[:, c, ts(tc, TCW)] for c in range(NC_)]
            hooks = {}
            if tc + 1 < NTC:
                srcs_n = [X[:, c, ts(tc + 1, TCW)] for c in range(NC_)]
                pm = emit_ln_pre(srcs_n, Xb1, sq1)

                def mk_hooks(pm=pm, tcn=tc + 1):
                    def h0():
                        emit_ln_stats(pm, Xb1, sq1)

                    def h2():
                        lnms[tcn] = emit_ln_fin(Xb1, sq1, LN1[:, tcn % 2, :, :])
                    return {0: h0, 2: h2}
                hooks = mk_hooks()
            emit_ffn(LN1[:, tc % 2, :, :], lnms[tc], srcs, w1buf, w2buf, hT, hooks)

    if stage in ("ffn1", "xonly"):
        stflat = hT[:, 0:8, :].bitcast(F32).rearrange("p a b -> p (a b)")
        for tc in range(NTC):
            emit_store_chunk(X[:, :, ts(tc, TCW)], tc * TCW, stflat)
        SY.wait_ge(stq[0].h, 16 * st_state["cnt"][0])
        SY.wait_ge(stq[1].h, 16 * st_state["cnt"][1])
        blk_cm.__exit__(None, None, None)
        esP1.close()
        esX.close()
        es.close()
        return nc
    esP1.close()

    esP2 = contextlib.ExitStack()
    wq_s = esP2.enter_context(nc.sbuf_tensor("wq_s", [128, NC_, D], BF16))
    wk_s = esP2.enter_context(nc.sbuf_tensor("wk_s", [128, NC_, D], BF16))
    wv_s = esP2.enter_context(nc.sbuf_tensor("wv_s", [128, NC_, D], BF16))
    Xb2 = esP2.enter_context(nc.sbuf_tensor("Xb2", [128, NC_, TCW], BF16))
    sq2 = esP2.enter_context(nc.sbuf_tensor("sq2", [128, NC_, TCW], BF16))
    LN2 = esP2.enter_context(nc.sbuf_tensor("LN2", [128, NC_, TCW], BF16))
    evq = esP2.enter_context(nc.sbuf_tensor("evq", [128, 4, TCW], BF16))
    sqq = esP2.enter_context(nc.sbuf_tensor("sqq", [128, 2, TCW], BF16))

    GP.wait_ge(vfq.h, m_vfill)
    SY.wait_ge(sT.h, sT.n)
    SY.wait_ge(sA.h, sA.n)
    SY.wait_ge(sD.h, sD.n)
    SY.wait_ge(cvq.h, m_cvq)
    pwq.inc(SY.dma_start(out=wq_s[:, :, :], in_=wqbf[:, :, :]), 16)
    pwq.inc(SY.dma_start(out=wk_s[:, :, :], in_=wkbf[:, :, :]), 16)
    pwq.inc(SY.dma_start(out=wv_s[:, :, :], in_=wvbf[:, :, :]), 16)
    m_pwq = pwq.n
    GP.wait_ge(scq.h, m_scq)

    ev_state = [0] * 4
    evi = 0

    def evq_slot():
        nonlocal evi
        eb = evi % 4
        evi += 1
        if ev_state[eb] > 0:
            VE.wait_ge(evq_q[eb].h, 16 * ev_state[eb])
        return eb

    bank_ev = {0: 0, 1: 0, 2: 0, 3: 0}
    for tc in range(NTC):
        srcs = [X[:, c, ts(tc, TCW)] for c in range(NC_)]
        lnm = emit_ln(srcs, Xb2, sq2, LN2)
        TN.wait_ge(sD.h, lnm)
        if tc == 0:
            TN.wait_ge(pwq.h, m_pwq)
        for which in ("q", "k"):
            wsb = wq_s if which == "q" else wk_s
            brow = bqr if which == "q" else bkr
            tp = (0, 0) if which == "q" else (0, 32)
            dstq = bank(7)[0:16, :] if which == "q" else bank(7)[32:48, :]
            mm = {}
            sq_marks = {}
            st_marks = {}
            for p in range(NP):
                b = p % 2
                if bank_ev[b]:
                    TN.wait_ge(sD.h, bank_ev[b])
                for c in range(NC_):
                    TN.matmul(bank(b)[:, :], wsb[:, c, ts(p, 128)], LN2[:, c, :],
                              start=(c == 0), stop=False, tile_position=(0, 0), skip_group_check=True)
                sT.inc(TN.matmul(bank(b)[:, :], brow[0:1, ts(p, 128)], onesrow[0:1, :],
                                 start=False, stop=True, tile_position=(0, 0), skip_group_check=True))
                mm[p] = sT.n
                if p >= 1:
                    TN.wait_ge(sD.h, sq_marks[p - 1])
                    sT.inc(TN.matmul(dstq, bdsb[:, p - 1, :], sqq[:, (p - 1) % 2, :],
                                     start=(p - 1 == 0), stop=False, tile_position=tp,
                                     skip_group_check=True))
                    st_marks[p - 1] = sT.n
                eb = evq_slot()
                VE.wait_ge(sT.h, mm[p])
                if which == "q":
                    sD.inc(VE.tensor_scalar(evq[:, eb, :], bank(b)[:, :], 0.125, None, ALU.mult))
                else:
                    sD.inc(VE.tensor_copy(evq[:, eb, :], bank(b)[:, :]))
                bank_ev[b] = sD.n
                if p >= 2:
                    VE.wait_ge(sT.h, st_marks[p - 2])
                VE.drain()
                sD.inc(VE.tensor_tensor(sqq[:, p % 2, :], evq[:, eb, :], evq[:, eb, :], ALU.mult))
                sq_marks[p] = sD.n
                GP.wait_ge(sD.h, bank_ev[b])
                if which == "q":
                    evq_q[eb].inc(GP.dma_start(out=qdram[p, :, ts(tc, TCW)], in_=evq[:, eb, :]), 16)
                else:
                    evq_q[eb].inc(GP.dma_start(out=kstage[p, :, ts(tc, TCW)], in_=evq[:, eb, :]), 16)
                ev_state[eb] += 1
            TN.wait_ge(sD.h, sq_marks[NP - 1])
            sT.inc(TN.matmul(dstq, bdsb[:, NP - 1, :], sqq[:, (NP - 1) % 2, :],
                             start=False, stop=True, tile_position=tp, skip_group_check=True))
            VE.wait_ge(sT.h, sT.n)
            if which == "q":
                sD.inc(VE.reduce_max(qmax2[0:16, tc:tc + 1], bank(7)[0:16, :], axis=AX.X))
            else:
                sD.inc(VE.reduce_max(kmaxt[32:48, tc:tc + 1], bank(7)[32:48, :], axis=AX.X))
            TN.wait_ge(sD.h, sD.n)
        for tt in range(4):
            gt = tc * 4 + tt
            for half in range(2):
                b = 2 + (tt * 2 + half) % 2
                if bank_ev[b]:
                    TN.wait_ge(sD.h, bank_ev[b])
                for c in range(NC_):
                    TN.matmul(bank(b)[:, :], LN2[:, c, ts(tt, 128)], wv_s[:, c, ts(half, 512)],
                              start=(c == 0), stop=False, tile_position=(0, 0), skip_group_check=True)
                sT.inc(TN.matmul(bank(b)[:, :], onesrow[0:1, 0:128], bvr[0:1, ts(half, 512)],
                                 start=False, stop=True, tile_position=(0, 0), skip_group_check=True))
                mmv = sT.n
                eb = evq_slot()
                VE.wait_ge(sT.h, mmv)
                sD.inc(VE.tensor_copy(evq[:, eb, :], bank(b)[:, :]))
                bank_ev[b] = sD.n
                GP.wait_ge(sD.h, bank_ev[b])
                for h2 in (0, 1):
                    vsrc = evq[:, eb, :].rearrange("p (pp r) -> p pp r", pp=4)[:, :, 64 * h2:64 * h2 + 64]
                    dstv = custom_ap(vstage[0, :, 0, :],
                                     [(2080, 128), (266240, 4), (1, 64)],
                                     4 * half * 266240 + gt * 130 + 65 * h2)
                    evq_q[eb].inc(GP.dma_start(out=dstv, in_=vsrc), 16)
                ev_state[eb] += 2
    m_evq_fin = [16 * c for c in ev_state]
    esP2.close()

    VE.tensor_tensor(kmaxt[32:48, 4:5], kmaxt[32:48, 0:1], kmaxt[32:48, 1:2], ALU.max)
    VE.drain()
    VE.tensor_tensor(kmaxt[32:48, 4:5], kmaxt[32:48, 4:5], kmaxt[32:48, 2:3], ALU.max)
    VE.drain()
    sD.inc(VE.tensor_tensor(kmaxt[32:48, 4:5], kmaxt[32:48, 4:5], kmaxt[32:48, 3:4], ALU.max))
    GP.wait_ge(sD.h, sD.n)
    for i in range(4):
        GP.wait_ge(evq_q[i].h, m_evq_fin[i])
    isodd = nc.values_load(ps2[0:1, 0:1], engines=[EngineType.Pool], min_val=0, max_val=1)
    iseven = nc.values_load(ps2[0:1, 1:2], engines=[EngineType.Pool], min_val=0, max_val=1)
    shq1.inc(GP.dma_start(out=kshk[0, :, :, :], in_=kstage[:, :, :], cond=iseven), 16)
    shq1.inc(GP.dma_start(out=kshk[1, :, :, :], in_=kstage[:, :, :], cond=isodd), 16)
    shq1.inc(GP.dma_start(out=kshv[0, :, :, :, :], in_=vstage[:, :, :, :], cond=iseven), 16)
    shq1.inc(GP.dma_start(out=kshv[1, :, :, :, :], in_=vstage[:, :, :, :], cond=isodd), 16)
    shq1.inc(GP.dma_start(out=kshm[0, :, :], in_=kmaxt[32:48, 4:5], cond=iseven), 16)
    shq1.inc(GP.dma_start(out=kshm[1, :, :], in_=kmaxt[32:48, 4:5], cond=isodd), 16)
    GP.wait_ge(shq1.h, shq1.n)
    sPrep.inc(GP.remote_sem_update_broadcast(sSig.h, sPrepL.h, rdests=[(0, 1)] + [None] * 7))
    GP.wait_ge(sPrep.h, sPrep.n)
    GP.trigger_dma(count=1)

    GP.wait_ge(sSig.h, 2)
    isodd2 = nc.values_load(ps2[0:1, 0:1], engines=[EngineType.Pool], min_val=0, max_val=1)
    iseven2 = nc.values_load(ps2[0:1, 1:2], engines=[EngineType.Pool], min_val=0, max_val=1)
    shq2.inc(GP.dma_start(out=kpar[:, :, :], in_=kshk[1, :, :, :], cond=iseven2), 16)
    shq2.inc(GP.dma_start(out=kpar[:, :, :], in_=kshk[0, :, :, :], cond=isodd2), 16)
    shq2.inc(GP.dma_start(out=vpar[:, :, :, :], in_=kshv[1, :, :, :, :], cond=iseven2), 16)
    shq2.inc(GP.dma_start(out=vpar[:, :, :, :], in_=kshv[0, :, :, :, :], cond=isodd2), 16)
    shq2.inc(GP.dma_start(out=kmaxt[32:48, 5:6], in_=kshm[1, :, :], cond=iseven2), 16)
    shq2.inc(GP.dma_start(out=kmaxt[32:48, 5:6], in_=kshm[0, :, :], cond=isodd2), 16)
    m_shq2 = shq2.n

    VE.wait_ge(shq2.h, m_shq2)
    sD.inc(VE.tensor_tensor(kmaxt[32:48, 5:6], kmaxt[32:48, 5:6], kmaxt[32:48, 4:5], ALU.max))
    GP.wait_ge(sD.h, sD.n)
    cvq.inc(GP.dma_start(out=kmax2b[0:16, 0:1], in_=kmaxt[32:48, 5:6]), 16)
    VE.wait_ge(cvq.h, cvq.n)
    sD.inc(VE.tensor_scalar(Bt[0:16, 0:4], qmax2[0:16, 0:4], kmax2b[0:16, 0:1], None, ALU.mult))
    SC.wait_ge(sD.h, sD.n)
    SC.activation(Bt[0:16, 4:8], Bt[0:16, 0:4], AF.Ln)
    SC.drain()
    sA.inc(SC.activation(Bt[0:16, 4:8], Bt[0:16, 4:8], AF.Exp, scale=0.5))
    VE.wait_ge(sA.h, sA.n)
    sD.inc(VE.tensor_scalar(Bt[0:16, 4:8], Bt[0:16, 4:8], relT[:, 32:33], None, ALU.add))
    GP.wait_ge(sD.h, sD.n)
    cvq.inc(GP.dma_start(out=bcrow[0:1, 0:64], in_=Bt[0:16, 4:8]), 16)
    VE.wait_ge(cvq.h, cvq.n)
    VE.drain()
    VE.tensor_tensor(custom_ap(bcrow[0:1, :], [(192, 1), (8, 8), (1, 4)], 64),
                     custom_ap(bcrow[0:1, :], [(192, 1), (8, 8), (1, 4)], 0),
                     custom_ap(bcrow[0:1, :], [(192, 1), (8, 8), (1, 4)], 4), ALU.max)
    VE.drain()
    sD.inc(VE.tensor_copy(custom_ap(bcrow[0:1, :], [(192, 1), (8, 8), (1, 4)], 68),
                          custom_ap(bcrow[0:1, :], [(192, 1), (8, 8), (1, 4)], 64)))
    GP.wait_ge(sD.h, sD.n)
    cvq.inc(GP.dma_start(out=Bt[0:16, 4:8], in_=bcrow[0:1, 64:128]), 16)
    VE.wait_ge(cvq.h, cvq.n)
    VE.drain()
    VE.tensor_scalar(Bt[0:16, 8:12], Bt[0:16, 4:8], -1.0, None, ALU.mult)
    VE.drain()
    VE.tensor_copy(bcpm[:, 0:4], Bt[0:16, 8:12])
    VE.tensor_scalar(bcpm[:, 4:8], Bt[0:16, 8:12], relT[:, 31:32], None, ALU.add)
    sD.inc(VE.tensor_scalar(bcpm[:, 8:12], Bt[0:16, 8:12], relT[:, 34:35], None, ALU.add))
    GP.wait_ge(sD.h, sD.n)
    cvq.inc(GP.dma_start(out=bcrow[0:1, 0:192], in_=bcpm[:, :]), 16)
    TN.wait_ge(cvq.h, cvq.n)
    sT.inc(TN.matmul(bank(2)[:, 0:192], onesf[0:1, 0:128], bcrow[0:1, :],
                     start=True, stop=True, tile_position=(0, 0)))
    VE.wait_ge(sT.h, sT.n)
    sD.inc(VE.tensor_copy(bcol[:, :], bank(2)[:, 0:192]))
    m_bcol = sD.n

    SY.wait_ge(sD.h, sD.n)
    for c in range(NC_):
        spq.inc(SY.dma_start(out=xdram[c, :, :], in_=X[:, c, :]), 16)
    m_spq = spq.n

    esVS = contextlib.ExitStack()
    vstg = esVS.enter_context(nc.sbuf_tensor("vstg", [128, 2, 16, 130], BF16))
    SC.wait_ge(sD.h, m_rel)
    sA.inc(SC.activation(relT[:, 35:36], relT[:, 34:35], AF.Exp))
    GP.wait_ge(sA.h, sA.n)
    vrq.inc(GP.dma_start(out=vscrow[0:1, 0:16], in_=relT[:, 35:36]), 16)
    TN.wait_ge(vrq.h, vrq.n)
    sT.inc(TN.matmul(bank(3)[:, 0:16], onesf[0:1, 0:128], vscrow[0:1, 0:16],
                     start=True, stop=True, tile_position=(0, 0)))
    VE.wait_ge(sT.h, sT.n)
    sD.inc(VE.tensor_copy(vsc16[:, :], bank(3)[:, 0:16]))
    m_vsc = sD.n
    for p in range(NP):
        if p == 0:
            GP.wait_ge(shq2.h, m_shq2)
        if p >= 2:
            GP.wait_ge(vss[p % 2].h, 16 * (p // 2))
        vsl[p % 2].inc(GP.dma_start(out=vstg[:, p % 2, :, :], in_=vpar[p, :, :, :]), 16)
        VE.wait_ge(vsl[p % 2].h, 16 * (p // 2 + 1))
        if p == 0:
            VE.wait_ge(sD.h, m_vsc)
        for h2 in (0, 1):
            sD.inc(VE.tensor_scalar(vstg[:, p % 2, :, 65 * h2:65 * h2 + 65],
                                    vstg[:, p % 2, :, 65 * h2:65 * h2 + 65],
                                    vsc16[:, 2 * p + h2:2 * p + h2 + 1], None, ALU.mult))
        SY.wait_ge(sD.h, sD.n)
        vss[p % 2].inc(SY.dma_start(out=vpar[p, :, :, :], in_=vstg[:, p % 2, :, :]), 16)
    m_vss = [vss[0].n, vss[1].n]
    esVS.close()
    esX.close()

    esP4 = contextlib.ExitStack()
    Kbuf = esP4.enter_context(nc.sbuf_tensor("Kbuf", [128, 2, 2, T], BF16))
    Vbuf = esP4.enter_context(nc.sbuf_tensor("Vbuf", [128, 2, 32, 130], BF16))
    Qbuf = esP4.enter_context(nc.sbuf_tensor("Qbuf", [128, 2, 2, QCW], BF16))
    Pbuf = esP4.enter_context(nc.sbuf_tensor("Pbuf", [128, 2, 2, 2, QCW], BF16))
    Cqc = esP4.enter_context(nc.sbuf_tensor("Cqc", [128, NC_, QCW], BF16))
    Xqc = esP4.enter_context(nc.sbuf_tensor("Xqc", [128, NC_, QCW], F32))
    rr = esP4.enter_context(nc.sbuf_tensor("rr", [128, QCW], F32))
    usb = esP4.enter_context(nc.sbuf_tensor("usb", [128, QCW], F32))
    csc = esP4.enter_context(nc.sbuf_tensor("csc", [128, 2, QCW], BF16))
    wobuf = esP4.enter_context(nc.sbuf_tensor("wobuf", [128, 2, D], BF16))
    LN3 = esP4.enter_context(nc.sbuf_tensor("LN3", [128, NC_, TCW], BF16))
    w1buf4 = esP4.enter_context(nc.sbuf_tensor("w1buf4", [128, 2, 2, 8, 128], BF16))
    w2buf4 = esP4.enter_context(nc.sbuf_tensor("w2buf4", [128, 2, 16, 128], BF16))
    hT4 = esP4.enter_context(nc.sbuf_tensor("hT4", [128, 32, TCW], BF16))
    stflat4 = hT4[:, 0:8, :].bitcast(F32).rearrange("p a b -> p (a b)")
    Xb3 = Cqc
    sq3 = Pbuf[:, :, :, :, :].rearrange("p a b c d -> p (a b c) d")

    cntA = [0, 0]
    cntB = [0, 0]
    xgate = [0, 0]
    pv15_mark = [0, 0]
    pv31_mark = [0, 0]
    qlast = {0: [0, 0], 1: [0, 0]}
    xc_cnt = [0]
    wo_cnt = [0, 0]
    wo_cons = [0, 0]

    def kslice(sl, j):
        if j < 16:
            return Kbuf[:, sl, 0, ts(j, 128)]
        return Kbuf[:, sl, 1, ts(j - 16, 128)]

    first_q = [True]

    def emit_A(nq, sl):
        qc_, quad_ = nq // 4, nq % 4
        p = 2 * quad_ + sl
        if cntA[sl] == 0:
            GP.wait_ge(shq2.h, m_shq2)
            GP.wait_ge(spq.h, m_spq)
            GP.wait_ge(vss[0].h, m_vss[0])
            GP.wait_ge(vss[1].h, m_vss[1])
            if first_q[0]:
                for i in range(4):
                    GP.wait_ge(evq_q[i].h, m_evq_fin[i])
                first_q[0] = False
        else:
            GP.wait_ge(sT.h, pv15_mark[sl])
        if qlast[sl][nq % 2]:
            GP.wait_ge(sT.h, qlast[sl][nq % 2])
        kvqA[sl].inc(GP.dma_start(out=Kbuf[:, sl, 0, :], in_=kstage[p, :, :]), 16)
        kvqA[sl].inc(GP.dma_start(out=Vbuf[:, sl, 0:16, :], in_=vstage[p, :, :, :]), 16)
        kvqA[sl].inc(GP.dma_start(out=Qbuf[:, sl, nq % 2, :], in_=qdram[p, :, ts(qc_, QCW)]), 16)
        cntA[sl] += 1

    def emit_B(nq, sl):
        p = 2 * (nq % 4) + sl
        if cntB[sl] > 0:
            GP.wait_ge(sT.h, pv31_mark[sl])
        kvqB[sl].inc(GP.dma_start(out=Kbuf[:, sl, 1, :], in_=kpar[p, :, :]), 16)
        kvqB[sl].inc(GP.dma_start(out=Vbuf[:, sl, 16:32, :], in_=vpar[p, :, :, :]), 16)
        cntB[sl] += 1

    def wo_load(c):
        slw = c % 2
        if wo_cons[slw] == 0:
            GP.wait_ge(sT.h, sT.n)
            GP.wait_ge(sD.h, sD.n)
        GP.wait_ge(sT.h, wo_cons[slw])
        woq[slw].inc(GP.dma_start(out=wobuf[:, slw, :], in_=wobf[c, :, :]), 16)
        wo_cnt[slw] += 1

    for sl in (0, 1):
        emit_A(0, sl)
    for qc in range(NQC):
        SY.wait_ge(spq.h, m_spq)
        SY.wait_ge(vss[0].h, m_vss[0])
        SY.wait_ge(vss[1].h, m_vss[1])
        if xc_cnt[0] > 0:
            SY.wait_ge(sT.h, xgate[0])
            SY.wait_ge(sD.h, xgate[1])
        else:
            SY.wait_ge(sT.h, sT.n)
            SY.wait_ge(sD.h, sD.n)
        for c in range(NC_):
            xcq.inc(SY.dma_start(out=Xqc[:, c, :], in_=xdram[c, :, ts(qc, QCW)]), 16)
        xc_cnt[0] += 1
        wo_base = [wo_cnt[0], wo_cnt[1]]
        wo_load(0)
        wo_load(1)
        for quad in range(4):
            nq = qc * 4 + quad
            for sl in (0, 1):
                emit_B(nq, sl)
                TN.wait_ge(kvqA[sl].h, 48 * cntA[sl])
            TN.wait_ge(sD.h, sD.n)
            TN.wait_ge(sA.h, sA.n)
            TN.wait_ge(bndq.h, m_bndq)
            exp_mark = {0: 0, 1: 0}
            pv_mark = {0: 0, 1: 0}
            pv_hist = {0: {}, 1: {}}

            def emit_S(j, sl):
                if j > 0:
                    TN.wait_ge(sA.h, exp_mark[sl])
                if j == 16:
                    TN.wait_ge(kvqB[sl].h, 32 * cntB[sl])
                TN.matmul(bank(2 * sl)[:, :], kslice(sl, j)[0:64, :], Qbuf[0:64, sl, nq % 2, :],
                          start=True, stop=True, tile_position=(0, 0))
                sT.inc(TN.matmul(bank(2 * sl + 1)[:, :], kslice(sl, j)[64:128, :],
                                 Qbuf[64:128, sl, nq % 2, :],
                                 start=True, stop=True, tile_position=(64, 0)))
                return sT.n

            def emit_exp(j, sl, mS):
                regs = own_regions(j, qc) if j < 16 else partner_regions(j, qc)
                need_band = any(r[4] > r[3] for r in regs)
                m_band_dve = 0
                if need_band:
                    VE.wait_ge(sT.h, mS)
                    for hh in (0, 1):
                        gh = 4 * quad + 2 * sl + hh
                        for (lo, hi, cls, blo, bhi) in regs:
                            if bhi <= blo:
                                continue
                            c0 = blo - qc * QCW
                            c1 = bhi - qc * QCW
                            if j < 16:
                                wlo = 128 * j
                                i0 = 239 - (blo - wlo)
                                i1 = 239 - (bhi - wlo)
                                bsrc = bandDr[:, gh, i0:(i1 if i1 >= 0 else None):-1]
                            else:
                                bsrc = bndDr[:, gh, 127::-1]
                            sD.inc(VE.tensor_tensor(bank(2 * sl + hh)[:, c0:c1],
                                                    bank(2 * sl + hh)[:, c0:c1], bsrc, ALU.add))
                    m_band_dve = sD.n
                SC.wait_ge(sT.h, mS)
                if need_band:
                    SC.wait_ge(sD.h, m_band_dve)
                if j > 1:
                    SC.wait_ge(sT.h, pv_hist[sl][j - 2])
                if len(regs) == 1 and regs[0][2] == 0 and not need_band:
                    bidx = (4 * quad + 2 * sl) * 12 + qc
                    sA.inc(SC.activation(Pbuf[:, sl, j % 2, :, :],
                                         pa[:, 2 * sl:2 * sl + 2, :],
                                         AF.Exp, bias=bcol[:, bidx:bidx + 1], scale=1.0))
                else:
                    for hh in (0, 1):
                        gh = 4 * quad + 2 * sl + hh
                        for (lo, hi, cls, blo, bhi) in regs:
                            c0 = lo - qc * QCW
                            c1 = hi - qc * QCW
                            bidx = gh * 12 + cls * 4 + qc
                            sA.inc(SC.activation(Pbuf[:, sl, j % 2, hh, c0:c1],
                                                 bank(2 * sl + hh)[:, c0:c1],
                                                 AF.Exp, bias=bcol[:, bidx:bidx + 1], scale=1.0))
                exp_mark[sl] = sA.n

            def emit_PV(j, sl):
                TN.wait_ge(sA.h, exp_mark[sl])
                first = (j == 0)
                last = (j == NKT - 1)
                for hh in (0, 1):
                    hq = 2 * sl + hh
                    ins = TN.matmul(bank(4 + hq)[0:65, :], Vbuf[:, sl, j, 65 * hh:65 * hh + 65],
                                    Pbuf[:, sl, j % 2, hh, :],
                                    start=first, stop=last, tile_position=(0, 0),
                                    skip_group_check=True)
                sT.inc(ins)
                pv_mark[sl] = sT.n
                pv_hist[sl][j] = sT.n

            for sl in (0, 1):
                mS = emit_S(0, sl)
                emit_exp(0, sl, mS)
            for j in range(1, NKT):
                if j == 17 and nq + 1 < 16:
                    for sl in (0, 1):
                        pv15_mark[sl] = pv_hist[sl][15]
                        emit_A(nq + 1, sl)
                for sl in (0, 1):
                    mS = emit_S(j, sl)
                    if j == NKT - 1:
                        qlast[sl][nq % 2] = mS
                    emit_PV(j - 1, sl)
                    emit_exp(j, sl, mS)
            for sl in (0, 1):
                emit_PV(NKT - 1, sl)
                pv31_mark[sl] = pv_hist[sl][NKT - 1]
            m_allpv = sT.n
            VE.wait_ge(sT.h, m_allpv)
            for hq in range(4):
                if hq > 0:
                    VE.wait_ge(sT.h, sT.n)
                sD.inc(VE.reciprocal(rr[64:65, :], bank(4 + hq)[64:65, :]))
                VE.drain()
                bb = hq % 2
                TN.wait_ge(sD.h, sD.n)
                sT.inc(TN.matmul(bank(bb)[0:64, :], onesf[64:65, 0:64], rr[64:65, :],
                                 start=True, stop=True, tile_position=(64, 0)))
                VE.wait_ge(sT.h, sT.n)
                sD.inc(VE.tensor_copy(usb[0:64, :], bank(bb)[0:64, :]))
                VE.drain()
                pidx = 2 * quad + hq // 2
                if hq % 2 == 0:
                    sD.inc(VE.tensor_tensor(Cqc[0:64, pidx, :], bank(4 + hq)[0:64, :],
                                            usb[0:64, :], ALU.mult))
                else:
                    if quad > 0:
                        VE.wait_ge(ccq.h, ccq.n)
                    sD.inc(VE.tensor_tensor(csc[0:64, hq // 2, :], bank(4 + hq)[0:64, :],
                                            usb[0:64, :], ALU.mult))
                    GP.wait_ge(sD.h, sD.n)
                    ccq.inc(GP.dma_start(out=Cqc[64:128, pidx, :], in_=csc[0:64, hq // 2, :]), 16)
                TN.wait_ge(sD.h, sD.n)
        m_attn = sD.n
        TN.wait_ge(sD.h, m_attn)
        TN.wait_ge(ccq.h, ccq.n)
        TN.wait_ge(xcq.h, 128 * xc_cnt[0])
        for c in range(NC_):
            slw = c % 2
            TN.wait_ge(woq[slw].h, 16 * (wo_base[slw] + c // 2 + 1))
            for d in range(8):
                ins = TN.matmul(bank(d)[:, :], wobuf[:, slw, ts(d, 128)], Cqc[:, c, :],
                                start=(c == 0), stop=False, tile_position=(0, 0),
                                skip_group_check=True)
            sT.inc(ins)
            wo_cons[slw] = sT.n
            if c + 2 < NC_:
                wo_load(c + 2)
        for d in range(8):
            sT.inc(TN.matmul(bank(d)[:, :], bor[0:1, ts(d, 128)], onesrow[0:1, :],
                             start=False, stop=True, tile_position=(0, 0), skip_group_check=True))
        m_oproj = sT.n
        VE.wait_ge(sT.h, m_oproj)
        for d in range(8):
            sD.inc(VE.scalar_tensor_tensor(Xqc[:, d, :], bank(d)[:, :], 1.0, Xqc[:, d, :],
                                           ALU.mult, ALU.add))
        SC.wait_ge(stq[0].h, 16 * st_state["cnt"][0])
        SC.wait_ge(stq[1].h, 16 * st_state["cnt"][1])
        srcs = [Xqc[:, c, :] for c in range(NC_)]
        lnm = emit_ln(srcs, Xb3, sq3, LN3)
        emit_ffn(LN3, lnm, srcs, w1buf4, w2buf4, hT4)
        emit_store_chunk(Xqc, qc * QCW, stflat4)
        xgate[0] = sT.n
        xgate[1] = sD.n

    SY.wait_ge(stq[0].h, 16 * st_state["cnt"][0])
    SY.wait_ge(stq[1].h, 16 * st_state["cnt"][1])
    blk_cm.__exit__(None, None, None)
    esP4.close()
    es.close()
    return nc


def kernel(**inputs):
    x = np.asarray(inputs["L0_attn_x"])
    wq = np.asarray(inputs["L0_attn_wq"])
    bq = np.asarray(inputs["L0_attn_bq"])
    wk = np.asarray(inputs["L0_attn_wk"])
    bk = np.asarray(inputs["L0_attn_bk"])
    wv = np.asarray(inputs["L0_attn_wv"])
    bv = np.asarray(inputs["L0_attn_bv"])
    wo = np.asarray(inputs["L0_attn_wo"])
    bo = np.asarray(inputs["L0_attn_bo"])
    relb = np.asarray(inputs["L0_attn_rel_bias"])
    W1 = np.asarray(inputs["L0_ffn_W1"])
    b1 = np.asarray(inputs["L0_ffn_b1"])
    W2 = np.asarray(inputs["L0_ffn_W2"])
    b2 = np.asarray(inputs["L0_ffn_b2"])

    cst = host_constants()
    shared = {
        "wqT": np.ascontiguousarray(wq.T), "wkT": np.ascontiguousarray(wk.T),
        "wvT": np.ascontiguousarray(wv.T), "woT": np.ascontiguousarray(wo.T),
        "W1": np.ascontiguousarray(W1), "W2": np.ascontiguousarray(W2),
        "bq": bq.reshape(1, -1), "bk": bk.reshape(1, -1), "bv": bv.reshape(1, -1),
        "bo": bo.reshape(1, -1), "b1": b1.reshape(1, -1), "b2": b2.reshape(1, -1),
        "relb": np.ascontiguousarray(relb),
        "relb0": np.ascontiguousarray(np.tile(relb[0:1, :], (32, 1))),
        "relbT": np.ascontiguousarray(relb.T),
        "OH": cst["OH"], "OHP": cst["OHP"], "ident": cst["ident"], "bd": cst["bd"],
    }
    in_maps = []
    for i in range(8):
        b_, h_ = i // 2, i % 2
        m = dict(shared)
        m["x"] = np.ascontiguousarray(x[b_, h_ * T:(h_ + 1) * T, :])
        m["par2"] = np.array([[h_, 1 - h_]], np.int32)
        m["parf"] = np.full((32, 1), float(h_), np.float32)
        in_maps.append(m)

    nc = build("full")
    mybir.codegen_inst_isa_subclasses(nc)
    res = run_bass_kernel_spmd(nc, in_maps, core_ids=list(range(8)))
    out = np.empty((4, 4096, 1024), np.float32)
    for i in range(8):
        b_, h_ = i // 2, i % 2
        out[b_, h_ * T:(h_ + 1) * T, :] = res.results[i]["y"]
    return out


if __name__ == "__main__":
    nc = build("full")
    print("instructions:", nc.n_instructions())


# version 09
# speedup vs baseline: 1.1562x; 1.1562x over previous
import contextlib
import numpy as np
import ml_dtypes
import bass_rust
import concourse.bass as bass
import concourse.mybir as mybir
from concourse.engine_type import EngineType
from concourse.bass_utils import run_bass_kernel_spmd
from concourse.library_config import proxy as proxy_lib

F32 = mybir.dt.float32
BF16 = mybir.dt.bfloat16
I32 = mybir.dt.int32
AF = mybir.ActivationFunctionType
ALU = mybir.AluOpType
AX = mybir.AxisListType
ts = bass.ts

D = 1024
T = 2048
IT = 4096
NC_ = 8
NTC = 4
TCW = 512
NP = 8
NKT = 32
NQC = 4
QCW = 512
EPS = 1e-5


def _bucket_np(e):
    n = np.asarray(e, np.int64)
    is_small = n < 16
    nf = np.maximum(n, 1).astype(np.float32)
    val = (np.log(nf / np.float32(16.0)).astype(np.float32)
           / np.float32(np.log(np.float32(8.0))) * np.float32(16.0))
    vl = 16 + val.astype(np.int32)
    vl = np.minimum(vl, 31)
    ret = np.where(is_small, n, vl)
    return np.maximum(ret, 0).astype(np.int64)


def host_constants():
    c = {}
    tt = np.arange(368)
    c["OH"] = (_bucket_np(239 - tt)[None, :] == np.arange(32)[:, None]).astype(ml_dtypes.bfloat16)
    tt2 = np.arange(256)
    c["OHP"] = (_bucket_np(255 - tt2)[None, :] == np.arange(32)[:, None]).astype(ml_dtypes.bfloat16)
    c["ident"] = np.eye(128, dtype=np.float32)
    bd = np.zeros((8, 128, 16), dtype=ml_dtypes.bfloat16)
    for p in range(8):
        bd[p, 0:64, 2 * p] = 1
        bd[p, 64:128, 2 * p + 1] = 1
    c["bd"] = bd
    return c


class Sem:
    def __init__(self, es, nc, name):
        self.h = es.enter_context(nc.semaphore(name))
        self.n = 0

    def inc(self, instr, k=1):
        instr.then_inc(self.h, k)
        self.n += k
        return instr


def custom_ap(src, pairs, offset):
    a = src
    a.ap = bass_rust.VecI64Pair(pairs)
    a.offset = offset
    return a


def own_regions(j, qc):
    a0, a1 = qc * QCW, qc * QCW + QCW
    wlo, whi = 128 * j, min(128 * j + 240, T)
    out = []
    hi0 = min(a1, whi)
    if hi0 > a0:
        blo, bhi = max(a0, wlo), min(hi0, whi)
        out.append((a0, hi0, 0, blo, max(bhi, blo)))
    lo1 = max(a0, whi)
    if a1 > lo1:
        out.append((lo1, a1, 1, 0, 0))
    return out


def partner_regions(j, qc):
    a0, a1 = qc * QCW, qc * QCW + QCW
    if j == 31 and qc == 0:
        return [(0, 128, 0, 0, 128), (128, QCW, 0, 0, 0)]
    return [(a0, a1, 0, 0, 0)]


def build(stage="full"):
    nc = bass.Bass("TRN2", target_bir_lowering=False)
    es = contextlib.ExitStack()

    dx = nc.dram_tensor("x", [T, D], F32, kind="ExternalInput").ap()
    dwqT = nc.dram_tensor("wqT", [D, D], F32, kind="ExternalInput").ap()
    dwkT = nc.dram_tensor("wkT", [D, D], F32, kind="ExternalInput").ap()
    dwvT = nc.dram_tensor("wvT", [D, D], F32, kind="ExternalInput").ap()
    dwoT = nc.dram_tensor("woT", [D, D], F32, kind="ExternalInput").ap()
    dW1 = nc.dram_tensor("W1", [D, IT], F32, kind="ExternalInput").ap()
    dW2 = nc.dram_tensor("W2", [IT, D], F32, kind="ExternalInput").ap()
    dbq = nc.dram_tensor("bq", [1, D], F32, kind="ExternalInput").ap()
    dbk = nc.dram_tensor("bk", [1, D], F32, kind="ExternalInput").ap()
    dbv = nc.dram_tensor("bv", [1, D], F32, kind="ExternalInput").ap()
    dbo = nc.dram_tensor("bo", [1, D], F32, kind="ExternalInput").ap()
    db1 = nc.dram_tensor("b1", [1, IT], F32, kind="ExternalInput").ap()
    db2 = nc.dram_tensor("b2", [1, D], F32, kind="ExternalInput").ap()
    drelb = nc.dram_tensor("relb", [32, 16], F32, kind="ExternalInput").ap()
    drelb0 = nc.dram_tensor("relb0", [32, 16], F32, kind="ExternalInput").ap()
    drelbT = nc.dram_tensor("relbT", [16, 32], F32, kind="ExternalInput").ap()
    dOH = nc.dram_tensor("OH", [32, 368], BF16, kind="ExternalInput").ap()
    dOHP = nc.dram_tensor("OHP", [32, 256], BF16, kind="ExternalInput").ap()
    dident = nc.dram_tensor("ident", [128, 128], F32, kind="ExternalInput").ap()
    dbd = nc.dram_tensor("bd", [8, 128, 16], BF16, kind="ExternalInput").ap()
    dpar2 = nc.dram_tensor("par2", [1, 2], I32, kind="ExternalInput").ap()
    dparf = nc.dram_tensor("parf", [32, 1], F32, kind="ExternalInput").ap()
    dy = nc.dram_tensor("y", [T, D], F32, kind="ExternalOutput").ap()

    w1bf = nc.dram_tensor("w1bf", [32, 128, 8, 128], BF16, kind="Internal").ap()
    w2bf = nc.dram_tensor("w2bf", [8, 128, 32, 128], BF16, kind="Internal").ap()
    wqbf = nc.dram_tensor("wqbf", [128, 8, 1024], BF16, kind="Internal").ap()
    wkbf = nc.dram_tensor("wkbf", [128, 8, 1024], BF16, kind="Internal").ap()
    wvbf = nc.dram_tensor("wvbf", [128, 8, 1024], BF16, kind="Internal").ap()
    wobf = nc.dram_tensor("wobf", [8, 128, 1024], BF16, kind="Internal").ap()
    qdram = nc.dram_tensor("qdram", [NP, 128, T], BF16, kind="Internal").ap()
    xdram = nc.dram_tensor("xdram", [NC_, 128, T], F32, kind="Internal").ap()
    frevd = nc.dram_tensor("frevd", [16, 368], F32, kind="Internal").ap()
    frevpd = nc.dram_tensor("frevpd", [16, 256], F32, kind="Internal").ap()
    kstage = nc.dram_tensor("kstage", [NP, 128, T], BF16, kind="Internal").ap()
    vstage = nc.dram_tensor("vstage", [NP, 128, 16, 130], BF16, kind="Internal").ap()
    kpar = nc.dram_tensor("kpar", [NP, 128, T], BF16, kind="Internal").ap()
    vpar = nc.dram_tensor("vpar", [NP, 128, 16, 130], BF16, kind="Internal").ap()
    kshk = nc.dram_tensor("kshk", [2, NP, 128, T], BF16, kind="Internal", addr_space="Shared").ap()
    kshv = nc.dram_tensor("kshv", [2, NP, 128, 16, 130], BF16, kind="Internal", addr_space="Shared").ap()
    kshm = nc.dram_tensor("kshm", [2, 16, 1], F32, kind="Internal", addr_space="Shared").ap()

    sT = Sem(es, nc, "sT")
    sA = Sem(es, nc, "sA")
    sD = Sem(es, nc, "sD")
    sG = Sem(es, nc, "sG")
    sSig = Sem(es, nc, "sSig")
    sPrep = Sem(es, nc, "sPrep")
    sPrepL = Sem(es, nc, "sPrepL")
    cvq = Sem(es, nc, "cvq")
    scq = Sem(es, nc, "scq")
    xq = [Sem(es, nc, f"xq{i}") for i in range(2)]
    spq = Sem(es, nc, "spq")
    ftq = Sem(es, nc, "ftq")
    bndq = Sem(es, nc, "bndq")
    shq1 = Sem(es, nc, "shq1")
    shq2 = Sem(es, nc, "shq2")
    pwq = Sem(es, nc, "pwq")
    evq_q = [Sem(es, nc, f"evq{i}") for i in range(4)]
    kvqA = [Sem(es, nc, f"kvqA{i}") for i in range(2)]
    kvqB = [Sem(es, nc, f"kvqB{i}") for i in range(2)]
    xcq = Sem(es, nc, "xcq")
    woq = [Sem(es, nc, f"woq{i}") for i in range(2)]
    ccq = Sem(es, nc, "ccq")
    vfq = Sem(es, nc, "vfq")
    stq = [Sem(es, nc, f"stq{i}") for i in range(2)]
    w1q = [Sem(es, nc, f"w1q{i}") for i in range(2)]
    w2q = [Sem(es, nc, f"w2q{i}") for i in range(2)]
    vsl = [Sem(es, nc, f"vsl{i}") for i in range(2)]
    vss = [Sem(es, nc, f"vss{i}") for i in range(2)]
    vrq = Sem(es, nc, "vrq")

    sbt = lambda name, shape, dt: es.enter_context(nc.sbuf_tensor(name, shape, dt))
    ident = sbt("identS", [128, 128], F32)
    onescol = sbt("onescol", [128, 1], BF16)
    onesf = sbt("onesf", [128, 128], F32)
    onesrow = sbt("onesrow", [1, 512], BF16)
    bqr = sbt("bqr", [1, D], BF16)
    bkr = sbt("bkr", [1, D], BF16)
    bvr = sbt("bvr", [1, D], BF16)
    bor = sbt("bor", [1, D], BF16)
    b1r = sbt("b1r", [1, IT], BF16)
    b2r = sbt("b2r", [1, D], BF16)
    bdsb = sbt("bdsb", [128, 8, 16], BF16)
    ps2 = sbt("ps2", [1, 2], I32)
    bandDr = sbt("bandDr", [128, 16, 240], F32)
    bndDr = sbt("bndDr", [128, 16, 128], F32)
    bcol = sbt("bcol", [128, 192], F32)
    qmax2 = sbt("qmax2", [128, 4], F32)
    kmaxt = sbt("kmaxt", [128, 6], F32)
    kmax2b = sbt("kmax2b", [128, 1], F32)
    lnrow = sbt("lnrow", [1, 8, 512], F32)
    rstdb = sbt("rstdb", [128, 512], BF16)
    mrsb = sbt("mrsb", [128, 512], BF16)
    relsb = sbt("relsb", [32, 64], F32)
    relbpb = sbt("relbpb", [32, 32], BF16)
    relT = sbt("relT", [16, 40], F32)
    Bt = sbt("Bt", [128, 16], F32)
    bcpm = sbt("bcpm", [16, 12], F32)
    bcrow = sbt("bcrow", [1, 192], F32)
    parfsb = sbt("parfsb", [32, 1], F32)
    vscrow = sbt("vscrow", [1, 16], F32)
    vsc16 = sbt("vsc16", [128, 16], F32)

    pa = es.enter_context(nc.psum_tensor("pa", [128, 8, 512], F32))

    def bank(i):
        return pa[:, i, :]

    blk_cm = nc.Block()
    blk = blk_cm.__enter__()
    TN, VE, SC, GP, SY = nc.tensor, nc.vector, nc.scalar, nc.gpsimd, nc.sync

    GP.load_library(proxy_lib)

    def emit_passthru():
        esmin = contextlib.ExitStack()
        xs0 = esmin.enter_context(nc.sbuf_tensor("xs0", [128, 2, D], F32))
        for tt in range(16):
            sl = tt % 2
            if tt >= 2:
                SY.wait_ge(stq[sl].h, 16 * (tt // 2))
            xq[sl].inc(SY.dma_start(out=xs0[:, sl, :], in_=dx[ts(tt, 128), :]), 16)
            SY.wait_ge(xq[sl].h, xq[sl].n)
            stq[sl].inc(SY.dma_start(out=dy[ts(tt, 128), :], in_=xs0[:, sl, :]), 16)
        SY.wait_ge(stq[0].h, 16 * 8)
        SY.wait_ge(stq[1].h, 16 * 8)
        blk_cm.__exit__(None, None, None)
        esmin.close()
        es.close()
        return nc

    if stage == "pt":
        return emit_passthru()
    for ii in range(32):
        cvq.inc(GP.dma_start(out=w1bf[ii, :, :, :],
                             in_=custom_ap(dW1[:, :], [(IT, 128), (128 * IT, 8), (1, 128)],
                                           ii * 128)), 16)
    for dd in range(8):
        for i4 in range(4):
            cvq.inc(GP.dma_start(out=w2bf[dd, :, 8 * i4:8 * i4 + 8, :],
                                 in_=custom_ap(dW2[:, :], [(D, 128), (128 * D, 8), (1, 128)],
                                               dd * 128 + 8 * i4 * 128 * D)), 16)
    for (wdst, wsrc) in ((wqbf, dwqT), (wkbf, dwkT), (wvbf, dwvT)):
        cvq.inc(GP.dma_start(out=wdst[:, :, :],
                             in_=custom_ap(wsrc[:, :], [(D, 128), (128 * D, 8), (1, 1024)], 0)), 16)
    cvq.inc(GP.dma_start(out=wobf[:, :, :], in_=dwoT.rearrange("(c p) d -> c p d", p=128)), 16)
    for (dst, src) in ((bqr, dbq), (bkr, dbk), (bvr, dbv), (bor, dbo), (b1r, db1), (b2r, db2)):
        cvq.inc(GP.dma_start(out=dst[:, :], in_=src[:, :]), 16)
    cvq.inc(GP.dma_start(out=bdsb[:, :, :], in_=dbd.rearrange("p r c -> r p c")), 16)
    cvq.inc(GP.dma_start(out=relsb[:, 0:16], in_=drelb[:, :]), 16)
    cvq.inc(GP.dma_start(out=relsb[:, 16:32], in_=drelb0[:, :]), 16)
    cvq.inc(GP.dma_start(out=relT[:, 0:32], in_=drelbT[:, :]), 16)
    m_cvq = cvq.n

    scq.inc(SY.dma_start(out=ident[:, :], in_=dident[:, :]), 16)
    scq.inc(SY.dma_start(out=ps2[:, :], in_=dpar2[:, :]), 16)
    scq.inc(SY.dma_start(out=parfsb[:, :], in_=dparf[:, :]), 16)
    m_scq = scq.n

    sG.inc(GP.memset(onescol[:, :], 1.0))
    sG.inc(GP.memset(onesf[:, :], 1.0))
    sG.inc(GP.memset(onesrow[:, :], 1.0))
    m_memset = sG.n
    TN.wait_ge(sG.h, m_memset)
    for _ in range(320):
        TN.matmul(bank(7)[:, :], onesrow[0:1, 0:128], onesrow[0:1, :],
                  start=True, stop=True, tile_position=(0, 0))
    esOB = contextlib.ExitStack()
    onesbig = esOB.enter_context(nc.sbuf_tensor("onesbig", [128, 2080], BF16))
    sG.inc(GP.memset(onesbig[:, :], 1.0))
    GP.wait_ge(sG.h, sG.n)
    for p in range(NP):
        vfq.inc(GP.dma_start(out=vstage[p, :, :, :], in_=onesbig[:, :].rearrange("p (a b) -> p a b", a=16)), 16)
    m_vfill = vfq.n
    esOB.close()

    if stage == "ptc":
        GP.wait_ge(cvq.h, m_cvq)
        return emit_passthru()

    esX = contextlib.ExitStack()
    X = esX.enter_context(nc.sbuf_tensor("X", [128, NC_, T], F32))
    esXs = contextlib.ExitStack()
    xstage = esXs.enter_context(nc.sbuf_tensor("xstage", [128, 2, D], F32))

    TN.wait_ge(scq.h, m_scq)
    VE.wait_ge(sG.h, m_memset)
    VE.wait_ge(vfq.h, m_vfill)
    SY.wait_ge(vfq.h, m_vfill)
    xtp_marks = []
    xev_marks = []
    for tt in range(16):
        sl = tt % 2
        if tt >= 2:
            SY.wait_ge(sT.h, xtp_marks[tt - 2])
        xq[sl].inc(SY.dma_start(out=xstage[:, sl, :], in_=dx[ts(tt, 128), :]), 16)
        TN.wait_ge(xq[sl].h, xq[sl].n)
        for cc in range(NC_):
            step = tt * NC_ + cc
            b = step % 4
            if step >= 4:
                TN.wait_ge(sD.h, xev_marks[step - 4])
            sT.inc(TN.transpose(bank(b)[:, 0:128], xstage[:, sl, ts(cc, 128)], ident[:, :]))
            VE.wait_ge(sT.h, sT.n)
            sD.inc(VE.tensor_copy(X[:, cc, ts(tt, 128)], bank(b)[:, 0:128]))
            xev_marks.append(sD.n)
        xtp_marks.append(sT.n)
    esXs.close()

    def emit_ln(src_slices, Xb, sq, LNout):
        VE.drain()
        for c in range(NC_):
            sD.inc(VE.tensor_copy(Xb[:, c, :], src_slices[c]))
        VE.drain()
        for c in range(NC_):
            sD.inc(VE.tensor_tensor(sq[:, c, :], Xb[:, c, :], Xb[:, c, :], ALU.mult))
        TN.wait_ge(sD.h, sD.n)
        for c in range(NC_):
            sT.inc(TN.matmul(bank(4)[0:1, :], onescol[:, 0:1], Xb[:, c, :],
                             start=(c == 0), stop=(c == NC_ - 1), tile_position=(0, 0),
                             skip_group_check=True))
        for c in range(NC_):
            sT.inc(TN.matmul(bank(5)[0:1, :], onescol[:, 0:1], sq[:, c, :],
                             start=(c == 0), stop=(c == NC_ - 1), tile_position=(0, 0),
                             skip_group_check=True))
        VE.wait_ge(sT.h, sT.n)
        VE.tensor_scalar(lnrow[0:1, 0, :], bank(4)[0:1, :], 1.0 / D, None, ALU.mult)
        VE.tensor_scalar(lnrow[0:1, 1, :], bank(5)[0:1, :], 1.0 / D, None, ALU.mult)
        VE.drain()
        VE.tensor_tensor(lnrow[0:1, 2, :], lnrow[0:1, 0, :], lnrow[0:1, 0, :], ALU.mult)
        VE.drain()
        VE.tensor_tensor(lnrow[0:1, 3, :], lnrow[0:1, 1, :], lnrow[0:1, 2, :], ALU.subtract)
        VE.drain()
        sD.inc(VE.tensor_scalar(lnrow[0:1, 3, :], lnrow[0:1, 3, :], EPS, None, ALU.add))
        SC.wait_ge(sD.h, sD.n)
        SC.activation(lnrow[0:1, 6, :], lnrow[0:1, 3, :], AF.Ln)
        SC.drain()
        sA.inc(SC.activation(lnrow[0:1, 4, :], lnrow[0:1, 6, :], AF.Exp, scale=-0.5))
        VE.wait_ge(sA.h, sA.n)
        VE.drain()
        sD.inc(VE.tensor_tensor(lnrow[0:1, 5, :], lnrow[0:1, 0, :], lnrow[0:1, 4, :], ALU.mult))
        TN.wait_ge(sA.h, sA.n)
        sT.inc(TN.matmul(bank(6)[:, :], onesf[0:1, 0:128], lnrow[0:1, 4, :],
                         start=True, stop=True, tile_position=(0, 0)))
        VE.wait_ge(sT.h, sT.n)
        sD.inc(VE.tensor_copy(rstdb[:, :], bank(6)[:, :]))
        TN.wait_ge(sD.h, sD.n)
        sT.inc(TN.matmul(bank(6)[:, :], onesf[0:1, 0:128], lnrow[0:1, 5, :],
                         start=True, stop=True, tile_position=(0, 0)))
        VE.wait_ge(sT.h, sT.n)
        sD.inc(VE.tensor_copy(mrsb[:, :], bank(6)[:, :]))
        VE.drain()
        for c in range(NC_):
            VE.tensor_tensor(sq[:, c, :], Xb[:, c, :], rstdb[:, :], ALU.mult)
        VE.drain()
        for c in range(NC_):
            sD.inc(VE.tensor_tensor(LNout[:, c, :], sq[:, c, :], mrsb[:, :], ALU.subtract))
        return sD.n

    fst = {"w1g": 0, "w2g": 0, "h": [0, 0], "hlast": 0,
           "w1cons": [0, 0], "w2cons": [0, 0]}

    def w1_load(gi, w1buf):
        g = gi % 16 if False else None
        slq = w1q[gi % 2]
        GP.wait_ge(sT.h, fst["w1cons"][gi % 2])
        gg = gi - (gi // 16) * 16
        for iw in range(2):
            slq.inc(GP.dma_start(out=w1buf[:, gi % 2, iw, :, :],
                                 in_=w1bf[2 * gg + iw, :, :, :]), 16)

    def w2_load(di, w2buf):
        slq = w2q[di % 2]
        GP.wait_ge(sT.h, fst["w2cons"][di % 2])
        hs = di - (di // 16) * 16
        d, hf = hs // 2, hs % 2
        for iw in range(2):
            slq.inc(GP.dma_start(out=w2buf[:, di % 2, 8 * iw:8 * iw + 8, :],
                                 in_=w2bf[d, :, 16 * hf + 8 * iw:16 * hf + 8 * iw + 8, :]), 16)

    def emit_ffn(LNout, ln_mark, Xdst, w1buf, w2buf, hT):
        gbase = fst["w1g"]
        w1_load(gbase + 0, w1buf)
        w1_load(gbase + 1, w1buf)
        TN.wait_ge(sD.h, ln_mark)
        for i in range(32):
            gi = gbase + i // 2
            if i % 2 == 0:
                TN.wait_ge(w1q[gi % 2].h, 32 * (gi // 2 + 1))
            b = i % 2
            TN.wait_ge(sA.h, fst["h"][b])
            for c in range(NC_):
                TN.matmul(bank(b)[:, :], w1buf[:, gi % 2, i % 2, c, :], LNout[:, c, :],
                          start=(c == 0), stop=False, tile_position=(0, 0), skip_group_check=True)
            sT.inc(TN.matmul(bank(b)[:, :], b1r[0:1, ts(i, 128)], onesrow[0:1, :],
                             start=False, stop=True, tile_position=(0, 0), skip_group_check=True))
            if i % 2 == 1:
                fst["w1cons"][gi % 2] = sT.n
                if gi + 2 < gbase + 16:
                    w1_load(gi + 2, w1buf)
            SC.wait_ge(sT.h, sT.n)
            sA.inc(SC.activation(hT[:, i, :], bank(b)[:, :], AF.Relu))
            fst["h"][b] = sA.n
        fst["w1g"] += 16
        fst["hlast"] = sA.n
        dbase = fst["w2g"]
        w2_load(dbase + 0, w2buf)
        w2_load(dbase + 1, w2buf)
        ylast = [0, 0]
        for d in range(8):
            TN.wait_ge(sA.h, fst["hlast"])
            b = 2 + d % 2
            TN.wait_ge(sD.h, ylast[d % 2])
            for i in range(32):
                di = dbase + 2 * d + i // 16
                if i % 16 == 0:
                    TN.wait_ge(w2q[di % 2].h, 32 * (di // 2 + 1))
                TN.matmul(bank(b)[:, :], w2buf[:, di % 2, i % 16, :], hT[:, i, :],
                          start=(i == 0), stop=False, tile_position=(0, 0), skip_group_check=True)
                if i % 16 == 15:
                    fst["w2cons"][di % 2] = sT.n + 1
            sT.inc(TN.matmul(bank(b)[:, :], b2r[0:1, ts(d, 128)], onesrow[0:1, :],
                             start=False, stop=True, tile_position=(0, 0), skip_group_check=True))
            fst["w2cons"][(dbase + 2 * d) % 2] = sT.n
            fst["w2cons"][(dbase + 2 * d + 1) % 2] = sT.n
            for nx in (dbase + 2 * d + 2, dbase + 2 * d + 3):
                if nx < dbase + 16:
                    w2_load(nx, w2buf)
            VE.wait_ge(sT.h, sT.n)
            sD.inc(VE.scalar_tensor_tensor(Xdst[d], bank(b)[:, :], 0.5, Xdst[d], ALU.mult, ALU.add))
            ylast[d % 2] = sD.n
        fst["w2g"] += 16
        return sD.n

    st_state = {"n": 0, "marks": {}, "cnt": [0, 0]}

    def emit_store_chunk(Xsrc, tok0, stflat):
        TN.wait_ge(sD.h, sD.n)
        for tt in range(4):
            sl = tt % 2
            for cc in range(NC_):
                step = st_state["n"]
                st_state["n"] += 1
                b = step % 4
                if step >= 4:
                    TN.wait_ge(sD.h, st_state["marks"][step - 4])
                sT.inc(TN.transpose(bank(b)[:, 0:128], Xsrc[:, cc, ts(tt, 128)], ident[:, :]))
                VE.wait_ge(sT.h, sT.n)
                if cc == 0 and st_state["cnt"][sl] > 0:
                    VE.wait_ge(stq[sl].h, 16 * st_state["cnt"][sl])
                sD.inc(VE.tensor_copy(stflat[:, sl * 1024 + cc * 128: sl * 1024 + cc * 128 + 128],
                                      bank(b)[:, 0:128]))
                st_state["marks"][step] = sD.n
            SY.wait_ge(sD.h, sD.n)
            stq[sl].inc(SY.dma_start(out=dy[tok0 + tt * 128: tok0 + (tt + 1) * 128, :],
                                     in_=stflat[:, sl * 1024: sl * 1024 + 1024]), 16)
            st_state["cnt"][sl] += 1

    VE.wait_ge(cvq.h, m_cvq)
    VE.wait_ge(scq.h, m_scq)
    VE.tensor_tensor(relsb[:, 32:48], relsb[:, 0:16], relsb[:, 16:32], ALU.subtract)
    VE.drain()
    VE.tensor_scalar(relsb[:, 48:64], relsb[:, 32:48], parfsb[:, 0:1], None, ALU.mult)
    VE.drain()
    VE.tensor_copy(relbpb[:, 0:16], relsb[:, 32:48])
    sD.inc(VE.tensor_copy(relbpb[:, 16:32], relsb[:, 48:64]))
    VE.tensor_scalar(relT[:, 0:32], relT[:, 0:32], relT[:, 0:1], None, ALU.subtract)
    VE.drain()
    sD.inc(VE.reduce_max(relT[:, 32:33], relT[:, 0:32], axis=AX.X, apply_absolute_value=True))
    VE.drain()
    sD.inc(VE.tensor_scalar(relT[:, 34:35], relT[:, 31:32], parfsb[0:16, 0:1], None, ALU.mult))
    m_rel = sD.n

    esOH = contextlib.ExitStack()
    oh_s = esOH.enter_context(nc.sbuf_tensor("oh_s", [32, 368], BF16))
    ohp_s = esOH.enter_context(nc.sbuf_tensor("ohp_s", [32, 256], BF16))
    frow = esOH.enter_context(nc.sbuf_tensor("frow", [16, 368], F32))
    GP.wait_ge(sT.h, sT.n)
    GP.wait_ge(sD.h, sD.n)
    cvq.inc(GP.dma_start(out=oh_s[:, :], in_=dOHP[:, :] if False else dOH[:, :]), 16)
    cvq.inc(GP.dma_start(out=ohp_s[:, :], in_=dOHP[:, :]), 16)
    TN.wait_ge(cvq.h, cvq.n)
    TN.wait_ge(sD.h, sD.n)
    sT.inc(TN.matmul(bank(0)[0:16, 0:368], relbpb[:, 0:16], oh_s[:, :],
                     start=True, stop=True, tile_position=(0, 0)))
    VE.wait_ge(sT.h, sT.n)
    sD.inc(VE.tensor_copy(frow[:, :], bank(0)[0:16, 0:368]))
    SY.wait_ge(sD.h, sD.n)
    ftq.inc(SY.dma_start(out=frevd[:, :], in_=frow[:, :]), 16)
    TN.wait_ge(sD.h, sD.n)
    sT.inc(TN.matmul(bank(1)[0:16, 0:256], relbpb[:, 16:32], ohp_s[:, :],
                     start=True, stop=True, tile_position=(0, 0)))
    VE.wait_ge(sT.h, sT.n)
    VE.wait_ge(ftq.h, ftq.n)
    VE.tensor_copy(frow[:, 0:256], bank(1)[0:16, 0:256])
    VE.drain()
    sD.inc(VE.tensor_scalar(frow[:, 0:256], frow[:, 0:256], relT[:, 34:35], None, ALU.subtract))
    SY.wait_ge(sD.h, sD.n)
    ftq.inc(SY.dma_start(out=frevpd[:, :], in_=frow[:, 0:256]), 16)
    GP.wait_ge(ftq.h, ftq.n)
    for gh in range(16):
        bndq.inc(GP.dma_start(out=bandDr[:, gh, :],
                              in_=custom_ap(frevd[0:1, :], [(1, 128), (1, 240)], gh * 368)), 16)
        bndq.inc(GP.dma_start(out=bndDr[:, gh, :],
                              in_=custom_ap(frevpd[0:1, :], [(1, 128), (1, 128)], gh * 256)), 16)
    m_bndq = bndq.n
    esOH.close()
    VE.wait_ge(ftq.h, ftq.n)
    GP.wait_ge(ftq.h, ftq.n)

    esP1 = contextlib.ExitStack()
    Xb1 = esP1.enter_context(nc.sbuf_tensor("Xb1", [128, NC_, TCW], BF16))
    sq1 = esP1.enter_context(nc.sbuf_tensor("sq1", [128, NC_, TCW], BF16))
    LN1 = esP1.enter_context(nc.sbuf_tensor("LN1", [128, NC_, TCW], BF16))
    w1buf = esP1.enter_context(nc.sbuf_tensor("w1buf", [128, 2, 2, 8, 128], BF16))
    w2buf = esP1.enter_context(nc.sbuf_tensor("w2buf", [128, 2, 16, 128], BF16))
    hT = esP1.enter_context(nc.sbuf_tensor("hT", [128, 32, TCW], BF16))

    TN.wait_ge(cvq.h, m_cvq)
    TN.wait_ge(sG.h, m_memset)
    VE.wait_ge(sG.h, m_memset)
    SC.wait_ge(sG.h, m_memset)
    GP.wait_ge(cvq.h, m_cvq)
    if stage != "xonly":
        for tc in range(NTC):
            srcs = [X[:, c, ts(tc, TCW)] for c in range(NC_)]
            lnm = emit_ln(srcs, Xb1, sq1, LN1)
            emit_ffn(LN1, lnm, srcs, w1buf, w2buf, hT)

    if stage in ("ffn1", "xonly"):
        stflat = hT[:, 0:8, :].bitcast(F32).rearrange("p a b -> p (a b)")
        for tc in range(NTC):
            emit_store_chunk(X[:, :, ts(tc, TCW)], tc * TCW, stflat)
        SY.wait_ge(stq[0].h, 16 * st_state["cnt"][0])
        SY.wait_ge(stq[1].h, 16 * st_state["cnt"][1])
        blk_cm.__exit__(None, None, None)
        esP1.close()
        esX.close()
        es.close()
        return nc
    esP1.close()

    esP2 = contextlib.ExitStack()
    wq_s = esP2.enter_context(nc.sbuf_tensor("wq_s", [128, NC_, D], BF16))
    wk_s = esP2.enter_context(nc.sbuf_tensor("wk_s", [128, NC_, D], BF16))
    wv_s = esP2.enter_context(nc.sbuf_tensor("wv_s", [128, NC_, D], BF16))
    Xb2 = esP2.enter_context(nc.sbuf_tensor("Xb2", [128, NC_, TCW], BF16))
    sq2 = esP2.enter_context(nc.sbuf_tensor("sq2", [128, NC_, TCW], BF16))
    LN2 = esP2.enter_context(nc.sbuf_tensor("LN2", [128, NC_, TCW], BF16))
    evq = esP2.enter_context(nc.sbuf_tensor("evq", [128, 4, TCW], BF16))
    sqq = esP2.enter_context(nc.sbuf_tensor("sqq", [128, 2, TCW], BF16))

    GP.wait_ge(vfq.h, m_vfill)
    SY.wait_ge(sT.h, sT.n)
    SY.wait_ge(sA.h, sA.n)
    SY.wait_ge(sD.h, sD.n)
    SY.wait_ge(cvq.h, m_cvq)
    pwq.inc(SY.dma_start(out=wq_s[:, :, :], in_=wqbf[:, :, :]), 16)
    pwq.inc(SY.dma_start(out=wk_s[:, :, :], in_=wkbf[:, :, :]), 16)
    pwq.inc(SY.dma_start(out=wv_s[:, :, :], in_=wvbf[:, :, :]), 16)
    m_pwq = pwq.n
    GP.wait_ge(scq.h, m_scq)

    ev_state = [0] * 4
    evi = 0

    def evq_slot():
        nonlocal evi
        eb = evi % 4
        evi += 1
        if ev_state[eb] > 0:
            VE.wait_ge(evq_q[eb].h, 16 * ev_state[eb])
        return eb

    bank_ev = {0: 0, 1: 0, 2: 0, 3: 0}
    for tc in range(NTC):
        srcs = [X[:, c, ts(tc, TCW)] for c in range(NC_)]
        lnm = emit_ln(srcs, Xb2, sq2, LN2)
        TN.wait_ge(sD.h, lnm)
        if tc == 0:
            TN.wait_ge(pwq.h, m_pwq)
        for which in ("q", "k"):
            wsb = wq_s if which == "q" else wk_s
            brow = bqr if which == "q" else bkr
            tp = (0, 0) if which == "q" else (0, 32)
            dstq = bank(7)[0:16, :] if which == "q" else bank(7)[32:48, :]
            mm = {}
            sq_marks = {}
            st_marks = {}
            for p in range(NP):
                b = p % 2
                if bank_ev[b]:
                    TN.wait_ge(sD.h, bank_ev[b])
                for c in range(NC_):
                    TN.matmul(bank(b)[:, :], wsb[:, c, ts(p, 128)], LN2[:, c, :],
                              start=(c == 0), stop=False, tile_position=(0, 0), skip_group_check=True)
                sT.inc(TN.matmul(bank(b)[:, :], brow[0:1, ts(p, 128)], onesrow[0:1, :],
                                 start=False, stop=True, tile_position=(0, 0), skip_group_check=True))
                mm[p] = sT.n
                if p >= 1:
                    TN.wait_ge(sD.h, sq_marks[p - 1])
                    sT.inc(TN.matmul(dstq, bdsb[:, p - 1, :], sqq[:, (p - 1) % 2, :],
                                     start=(p - 1 == 0), stop=False, tile_position=tp,
                                     skip_group_check=True))
                    st_marks[p - 1] = sT.n
                eb = evq_slot()
                VE.wait_ge(sT.h, mm[p])
                if which == "q":
                    sD.inc(VE.tensor_scalar(evq[:, eb, :], bank(b)[:, :], 0.125, None, ALU.mult))
                else:
                    sD.inc(VE.tensor_copy(evq[:, eb, :], bank(b)[:, :]))
                bank_ev[b] = sD.n
                if p >= 2:
                    VE.wait_ge(sT.h, st_marks[p - 2])
                VE.drain()
                sD.inc(VE.tensor_tensor(sqq[:, p % 2, :], evq[:, eb, :], evq[:, eb, :], ALU.mult))
                sq_marks[p] = sD.n
                GP.wait_ge(sD.h, bank_ev[b])
                if which == "q":
                    evq_q[eb].inc(GP.dma_start(out=qdram[p, :, ts(tc, TCW)], in_=evq[:, eb, :]), 16)
                else:
                    evq_q[eb].inc(GP.dma_start(out=kstage[p, :, ts(tc, TCW)], in_=evq[:, eb, :]), 16)
                ev_state[eb] += 1
            TN.wait_ge(sD.h, sq_marks[NP - 1])
            sT.inc(TN.matmul(dstq, bdsb[:, NP - 1, :], sqq[:, (NP - 1) % 2, :],
                             start=False, stop=True, tile_position=tp, skip_group_check=True))
            VE.wait_ge(sT.h, sT.n)
            if which == "q":
                sD.inc(VE.reduce_max(qmax2[0:16, tc:tc + 1], bank(7)[0:16, :], axis=AX.X))
            else:
                sD.inc(VE.reduce_max(kmaxt[32:48, tc:tc + 1], bank(7)[32:48, :], axis=AX.X))
            TN.wait_ge(sD.h, sD.n)
        for tt in range(4):
            gt = tc * 4 + tt
            for half in range(2):
                b = 2 + (tt * 2 + half) % 2
                if bank_ev[b]:
                    TN.wait_ge(sD.h, bank_ev[b])
                for c in range(NC_):
                    TN.matmul(bank(b)[:, :], LN2[:, c, ts(tt, 128)], wv_s[:, c, ts(half, 512)],
                              start=(c == 0), stop=False, tile_position=(0, 0), skip_group_check=True)
                sT.inc(TN.matmul(bank(b)[:, :], onesrow[0:1, 0:128], bvr[0:1, ts(half, 512)],
                                 start=False, stop=True, tile_position=(0, 0), skip_group_check=True))
                mmv = sT.n
                eb = evq_slot()
                VE.wait_ge(sT.h, mmv)
                sD.inc(VE.tensor_copy(evq[:, eb, :], bank(b)[:, :]))
                bank_ev[b] = sD.n
                GP.wait_ge(sD.h, bank_ev[b])
                for h2 in (0, 1):
                    vsrc = evq[:, eb, :].rearrange("p (pp r) -> p pp r", pp=4)[:, :, 64 * h2:64 * h2 + 64]
                    dstv = custom_ap(vstage[0, :, 0, :],
                                     [(2080, 128), (266240, 4), (1, 64)],
                                     4 * half * 266240 + gt * 130 + 65 * h2)
                    evq_q[eb].inc(GP.dma_start(out=dstv, in_=vsrc), 16)
                ev_state[eb] += 2
    m_evq_fin = [16 * c for c in ev_state]
    esP2.close()

    VE.tensor_tensor(kmaxt[32:48, 4:5], kmaxt[32:48, 0:1], kmaxt[32:48, 1:2], ALU.max)
    VE.drain()
    VE.tensor_tensor(kmaxt[32:48, 4:5], kmaxt[32:48, 4:5], kmaxt[32:48, 2:3], ALU.max)
    VE.drain()
    sD.inc(VE.tensor_tensor(kmaxt[32:48, 4:5], kmaxt[32:48, 4:5], kmaxt[32:48, 3:4], ALU.max))
    GP.wait_ge(sD.h, sD.n)
    for i in range(4):
        GP.wait_ge(evq_q[i].h, m_evq_fin[i])
    isodd = nc.values_load(ps2[0:1, 0:1], engines=[EngineType.Pool], min_val=0, max_val=1)
    iseven = nc.values_load(ps2[0:1, 1:2], engines=[EngineType.Pool], min_val=0, max_val=1)
    shq1.inc(GP.dma_start(out=kshk[0, :, :, :], in_=kstage[:, :, :], cond=iseven), 16)
    shq1.inc(GP.dma_start(out=kshk[1, :, :, :], in_=kstage[:, :, :], cond=isodd), 16)
    shq1.inc(GP.dma_start(out=kshv[0, :, :, :, :], in_=vstage[:, :, :, :], cond=iseven), 16)
    shq1.inc(GP.dma_start(out=kshv[1, :, :, :, :], in_=vstage[:, :, :, :], cond=isodd), 16)
    shq1.inc(GP.dma_start(out=kshm[0, :, :], in_=kmaxt[32:48, 4:5], cond=iseven), 16)
    shq1.inc(GP.dma_start(out=kshm[1, :, :], in_=kmaxt[32:48, 4:5], cond=isodd), 16)
    GP.wait_ge(shq1.h, shq1.n)
    sPrep.inc(GP.remote_sem_update_broadcast(sSig.h, sPrepL.h, rdests=[(0, 1)] + [None] * 7))
    GP.wait_ge(sPrep.h, sPrep.n)
    GP.trigger_dma(count=1)

    GP.wait_ge(sSig.h, 2)
    isodd2 = nc.values_load(ps2[0:1, 0:1], engines=[EngineType.Pool], min_val=0, max_val=1)
    iseven2 = nc.values_load(ps2[0:1, 1:2], engines=[EngineType.Pool], min_val=0, max_val=1)
    shq2.inc(GP.dma_start(out=kpar[:, :, :], in_=kshk[1, :, :, :], cond=iseven2), 16)
    shq2.inc(GP.dma_start(out=kpar[:, :, :], in_=kshk[0, :, :, :], cond=isodd2), 16)
    shq2.inc(GP.dma_start(out=vpar[:, :, :, :], in_=kshv[1, :, :, :, :], cond=iseven2), 16)
    shq2.inc(GP.dma_start(out=vpar[:, :, :, :], in_=kshv[0, :, :, :, :], cond=isodd2), 16)
    shq2.inc(GP.dma_start(out=kmaxt[32:48, 5:6], in_=kshm[1, :, :], cond=iseven2), 16)
    shq2.inc(GP.dma_start(out=kmaxt[32:48, 5:6], in_=kshm[0, :, :], cond=isodd2), 16)
    m_shq2 = shq2.n

    VE.wait_ge(shq2.h, m_shq2)
    sD.inc(VE.tensor_tensor(kmaxt[32:48, 5:6], kmaxt[32:48, 5:6], kmaxt[32:48, 4:5], ALU.max))
    GP.wait_ge(sD.h, sD.n)
    cvq.inc(GP.dma_start(out=kmax2b[0:16, 0:1], in_=kmaxt[32:48, 5:6]), 16)
    VE.wait_ge(cvq.h, cvq.n)
    sD.inc(VE.tensor_scalar(Bt[0:16, 0:4], qmax2[0:16, 0:4], kmax2b[0:16, 0:1], None, ALU.mult))
    SC.wait_ge(sD.h, sD.n)
    SC.activation(Bt[0:16, 4:8], Bt[0:16, 0:4], AF.Ln)
    SC.drain()
    sA.inc(SC.activation(Bt[0:16, 4:8], Bt[0:16, 4:8], AF.Exp, scale=0.5))
    VE.wait_ge(sA.h, sA.n)
    sD.inc(VE.tensor_scalar(Bt[0:16, 4:8], Bt[0:16, 4:8], relT[:, 32:33], None, ALU.add))
    GP.wait_ge(sD.h, sD.n)
    cvq.inc(GP.dma_start(out=bcrow[0:1, 0:64], in_=Bt[0:16, 4:8]), 16)
    VE.wait_ge(cvq.h, cvq.n)
    VE.drain()
    VE.tensor_tensor(custom_ap(bcrow[0:1, :], [(192, 1), (8, 8), (1, 4)], 64),
                     custom_ap(bcrow[0:1, :], [(192, 1), (8, 8), (1, 4)], 0),
                     custom_ap(bcrow[0:1, :], [(192, 1), (8, 8), (1, 4)], 4), ALU.max)
    VE.drain()
    sD.inc(VE.tensor_copy(custom_ap(bcrow[0:1, :], [(192, 1), (8, 8), (1, 4)], 68),
                          custom_ap(bcrow[0:1, :], [(192, 1), (8, 8), (1, 4)], 64)))
    GP.wait_ge(sD.h, sD.n)
    cvq.inc(GP.dma_start(out=Bt[0:16, 4:8], in_=bcrow[0:1, 64:128]), 16)
    VE.wait_ge(cvq.h, cvq.n)
    VE.drain()
    VE.tensor_scalar(Bt[0:16, 8:12], Bt[0:16, 4:8], -1.0, None, ALU.mult)
    VE.drain()
    VE.tensor_copy(bcpm[:, 0:4], Bt[0:16, 8:12])
    VE.tensor_scalar(bcpm[:, 4:8], Bt[0:16, 8:12], relT[:, 31:32], None, ALU.add)
    sD.inc(VE.tensor_scalar(bcpm[:, 8:12], Bt[0:16, 8:12], relT[:, 34:35], None, ALU.add))
    GP.wait_ge(sD.h, sD.n)
    cvq.inc(GP.dma_start(out=bcrow[0:1, 0:192], in_=bcpm[:, :]), 16)
    TN.wait_ge(cvq.h, cvq.n)
    sT.inc(TN.matmul(bank(2)[:, 0:192], onesf[0:1, 0:128], bcrow[0:1, :],
                     start=True, stop=True, tile_position=(0, 0)))
    VE.wait_ge(sT.h, sT.n)
    sD.inc(VE.tensor_copy(bcol[:, :], bank(2)[:, 0:192]))
    m_bcol = sD.n

    SY.wait_ge(sD.h, sD.n)
    for c in range(NC_):
        spq.inc(SY.dma_start(out=xdram[c, :, :], in_=X[:, c, :]), 16)
    m_spq = spq.n

    esVS = contextlib.ExitStack()
    vstg = esVS.enter_context(nc.sbuf_tensor("vstg", [128, 2, 16, 130], BF16))
    SC.wait_ge(sD.h, m_rel)
    sA.inc(SC.activation(relT[:, 35:36], relT[:, 34:35], AF.Exp))
    GP.wait_ge(sA.h, sA.n)
    vrq.inc(GP.dma_start(out=vscrow[0:1, 0:16], in_=relT[:, 35:36]), 16)
    TN.wait_ge(vrq.h, vrq.n)
    sT.inc(TN.matmul(bank(3)[:, 0:16], onesf[0:1, 0:128], vscrow[0:1, 0:16],
                     start=True, stop=True, tile_position=(0, 0)))
    VE.wait_ge(sT.h, sT.n)
    sD.inc(VE.tensor_copy(vsc16[:, :], bank(3)[:, 0:16]))
    m_vsc = sD.n
    for p in range(NP):
        if p == 0:
            GP.wait_ge(shq2.h, m_shq2)
        if p >= 2:
            GP.wait_ge(vss[p % 2].h, 16 * (p // 2))
        vsl[p % 2].inc(GP.dma_start(out=vstg[:, p % 2, :, :], in_=vpar[p, :, :, :]), 16)
        VE.wait_ge(vsl[p % 2].h, 16 * (p // 2 + 1))
        if p == 0:
            VE.wait_ge(sD.h, m_vsc)
        for h2 in (0, 1):
            sD.inc(VE.tensor_scalar(vstg[:, p % 2, :, 65 * h2:65 * h2 + 65],
                                    vstg[:, p % 2, :, 65 * h2:65 * h2 + 65],
                                    vsc16[:, 2 * p + h2:2 * p + h2 + 1], None, ALU.mult))
        SY.wait_ge(sD.h, sD.n)
        vss[p % 2].inc(SY.dma_start(out=vpar[p, :, :, :], in_=vstg[:, p % 2, :, :]), 16)
    m_vss = [vss[0].n, vss[1].n]
    esVS.close()
    esX.close()

    esP4 = contextlib.ExitStack()
    Kbuf = esP4.enter_context(nc.sbuf_tensor("Kbuf", [128, 2, 2, T], BF16))
    Vbuf = esP4.enter_context(nc.sbuf_tensor("Vbuf", [128, 2, 32, 130], BF16))
    Qbuf = esP4.enter_context(nc.sbuf_tensor("Qbuf", [128, 2, 2, QCW], BF16))
    Pbuf = esP4.enter_context(nc.sbuf_tensor("Pbuf", [128, 2, 2, 2, QCW], BF16))
    Cqc = esP4.enter_context(nc.sbuf_tensor("Cqc", [128, NC_, QCW], BF16))
    Xqc = esP4.enter_context(nc.sbuf_tensor("Xqc", [128, NC_, QCW], F32))
    rr = esP4.enter_context(nc.sbuf_tensor("rr", [128, QCW], F32))
    usb = esP4.enter_context(nc.sbuf_tensor("usb", [128, QCW], F32))
    csc = esP4.enter_context(nc.sbuf_tensor("csc", [128, 2, QCW], BF16))
    wobuf = esP4.enter_context(nc.sbuf_tensor("wobuf", [128, 2, D], BF16))
    LN3 = esP4.enter_context(nc.sbuf_tensor("LN3", [128, NC_, TCW], BF16))
    w1buf4 = esP4.enter_context(nc.sbuf_tensor("w1buf4", [128, 2, 2, 8, 128], BF16))
    w2buf4 = esP4.enter_context(nc.sbuf_tensor("w2buf4", [128, 2, 16, 128], BF16))
    hT4 = esP4.enter_context(nc.sbuf_tensor("hT4", [128, 32, TCW], BF16))
    stflat4 = hT4[:, 0:8, :].bitcast(F32).rearrange("p a b -> p (a b)")
    Xb3 = Cqc
    sq3 = Pbuf[:, :, :, :, :].rearrange("p a b c d -> p (a b c) d")

    cntA = [0, 0]
    cntB = [0, 0]
    xgate = [0, 0]
    pv15_mark = [0, 0]
    pv31_mark = [0, 0]
    qlast = {0: [0, 0], 1: [0, 0]}
    xc_cnt = [0]
    wo_cnt = [0, 0]
    wo_cons = [0, 0]

    def kslice(sl, j):
        if j < 16:
            return Kbuf[:, sl, 0, ts(j, 128)]
        return Kbuf[:, sl, 1, ts(j - 16, 128)]

    first_q = [True]

    def emit_A(nq, sl):
        qc_, quad_ = nq // 4, nq % 4
        p = 2 * quad_ + sl
        if cntA[sl] == 0:
            GP.wait_ge(shq2.h, m_shq2)
            GP.wait_ge(spq.h, m_spq)
            GP.wait_ge(vss[0].h, m_vss[0])
            GP.wait_ge(vss[1].h, m_vss[1])
            if first_q[0]:
                for i in range(4):
                    GP.wait_ge(evq_q[i].h, m_evq_fin[i])
                first_q[0] = False
        else:
            GP.wait_ge(sT.h, pv15_mark[sl])
        if qlast[sl][nq % 2]:
            GP.wait_ge(sT.h, qlast[sl][nq % 2])
        kvqA[sl].inc(GP.dma_start(out=Kbuf[:, sl, 0, :], in_=kstage[p, :, :]), 16)
        kvqA[sl].inc(GP.dma_start(out=Vbuf[:, sl, 0:16, :], in_=vstage[p, :, :, :]), 16)
        kvqA[sl].inc(GP.dma_start(out=Qbuf[:, sl, nq % 2, :], in_=qdram[p, :, ts(qc_, QCW)]), 16)
        cntA[sl] += 1

    def emit_B(nq, sl):
        p = 2 * (nq % 4) + sl
        if cntB[sl] > 0:
            GP.wait_ge(sT.h, pv31_mark[sl])
        kvqB[sl].inc(GP.dma_start(out=Kbuf[:, sl, 1, :], in_=kpar[p, :, :]), 16)
        kvqB[sl].inc(GP.dma_start(out=Vbuf[:, sl, 16:32, :], in_=vpar[p, :, :, :]), 16)
        cntB[sl] += 1

    def wo_load(c):
        slw = c % 2
        if wo_cons[slw] == 0:
            GP.wait_ge(sT.h, sT.n)
            GP.wait_ge(sD.h, sD.n)
        GP.wait_ge(sT.h, wo_cons[slw])
        woq[slw].inc(GP.dma_start(out=wobuf[:, slw, :], in_=wobf[c, :, :]), 16)
        wo_cnt[slw] += 1

    for sl in (0, 1):
        emit_A(0, sl)
    for qc in range(NQC):
        SY.wait_ge(spq.h, m_spq)
        SY.wait_ge(vss[0].h, m_vss[0])
        SY.wait_ge(vss[1].h, m_vss[1])
        if xc_cnt[0] > 0:
            SY.wait_ge(sT.h, xgate[0])
            SY.wait_ge(sD.h, xgate[1])
        else:
            SY.wait_ge(sT.h, sT.n)
            SY.wait_ge(sD.h, sD.n)
        for c in range(NC_):
            xcq.inc(SY.dma_start(out=Xqc[:, c, :], in_=xdram[c, :, ts(qc, QCW)]), 16)
        xc_cnt[0] += 1
        wo_base = [wo_cnt[0], wo_cnt[1]]
        wo_load(0)
        wo_load(1)
        for quad in range(4):
            nq = qc * 4 + quad
            for sl in (0, 1):
                emit_B(nq, sl)
                TN.wait_ge(kvqA[sl].h, 48 * cntA[sl])
            TN.wait_ge(sD.h, sD.n)
            TN.wait_ge(sA.h, sA.n)
            TN.wait_ge(bndq.h, m_bndq)
            exp_mark = {0: 0, 1: 0}
            pv_mark = {0: 0, 1: 0}
            pv_hist = {0: {}, 1: {}}

            def emit_S(j, sl):
                if j > 0:
                    TN.wait_ge(sA.h, exp_mark[sl])
                if j == 16:
                    TN.wait_ge(kvqB[sl].h, 32 * cntB[sl])
                TN.matmul(bank(2 * sl)[:, :], kslice(sl, j)[0:64, :], Qbuf[0:64, sl, nq % 2, :],
                          start=True, stop=True, tile_position=(0, 0))
                sT.inc(TN.matmul(bank(2 * sl + 1)[:, :], kslice(sl, j)[64:128, :],
                                 Qbuf[64:128, sl, nq % 2, :],
                                 start=True, stop=True, tile_position=(64, 0)))
                return sT.n

            def emit_exp(j, sl, mS):
                regs = own_regions(j, qc) if j < 16 else partner_regions(j, qc)
                need_band = any(r[4] > r[3] for r in regs)
                m_band_dve = 0
                if need_band:
                    VE.wait_ge(sT.h, mS)
                    for hh in (0, 1):
                        gh = 4 * quad + 2 * sl + hh
                        for (lo, hi, cls, blo, bhi) in regs:
                            if bhi <= blo:
                                continue
                            c0 = blo - qc * QCW
                            c1 = bhi - qc * QCW
                            if j < 16:
                                wlo = 128 * j
                                i0 = 239 - (blo - wlo)
                                i1 = 239 - (bhi - wlo)
                                bsrc = bandDr[:, gh, i0:(i1 if i1 >= 0 else None):-1]
                            else:
                                bsrc = bndDr[:, gh, 127::-1]
                            sD.inc(VE.tensor_tensor(bank(2 * sl + hh)[:, c0:c1],
                                                    bank(2 * sl + hh)[:, c0:c1], bsrc, ALU.add))
                    m_band_dve = sD.n
                SC.wait_ge(sT.h, mS)
                if need_band:
                    SC.wait_ge(sD.h, m_band_dve)
                if j > 1:
                    SC.wait_ge(sT.h, pv_hist[sl][j - 2])
                if len(regs) == 1 and regs[0][2] == 0 and not need_band:
                    bidx = (4 * quad + 2 * sl) * 12 + qc
                    sA.inc(SC.activation(Pbuf[:, sl, j % 2, :, :],
                                         pa[:, 2 * sl:2 * sl + 2, :],
                                         AF.Exp, bias=bcol[:, bidx:bidx + 1], scale=1.0))
                else:
                    for hh in (0, 1):
                        gh = 4 * quad + 2 * sl + hh
                        for (lo, hi, cls, blo, bhi) in regs:
                            c0 = lo - qc * QCW
                            c1 = hi - qc * QCW
                            bidx = gh * 12 + cls * 4 + qc
                            sA.inc(SC.activation(Pbuf[:, sl, j % 2, hh, c0:c1],
                                                 bank(2 * sl + hh)[:, c0:c1],
                                                 AF.Exp, bias=bcol[:, bidx:bidx + 1], scale=1.0))
                exp_mark[sl] = sA.n

            def emit_PV(j, sl):
                TN.wait_ge(sA.h, exp_mark[sl])
                first = (j == 0)
                last = (j == NKT - 1)
                for hh in (0, 1):
                    hq = 2 * sl + hh
                    ins = TN.matmul(bank(4 + hq)[0:65, :], Vbuf[:, sl, j, 65 * hh:65 * hh + 65],
                                    Pbuf[:, sl, j % 2, hh, :],
                                    start=first, stop=last, tile_position=(0, 0),
                                    skip_group_check=True)
                sT.inc(ins)
                pv_mark[sl] = sT.n
                pv_hist[sl][j] = sT.n

            for sl in (0, 1):
                mS = emit_S(0, sl)
                emit_exp(0, sl, mS)
            for j in range(1, NKT):
                if j == 17 and nq + 1 < 16:
                    for sl in (0, 1):
                        pv15_mark[sl] = pv_hist[sl][15]
                        emit_A(nq + 1, sl)
                for sl in (0, 1):
                    mS = emit_S(j, sl)
                    if j == NKT - 1:
                        qlast[sl][nq % 2] = mS
                    emit_PV(j - 1, sl)
                    emit_exp(j, sl, mS)
            for sl in (0, 1):
                emit_PV(NKT - 1, sl)
                pv31_mark[sl] = pv_hist[sl][NKT - 1]
            m_allpv = sT.n
            VE.wait_ge(sT.h, m_allpv)
            for hq in range(4):
                if hq > 0:
                    VE.wait_ge(sT.h, sT.n)
                sD.inc(VE.reciprocal(rr[64:65, :], bank(4 + hq)[64:65, :]))
                VE.drain()
                bb = hq % 2
                TN.wait_ge(sD.h, sD.n)
                sT.inc(TN.matmul(bank(bb)[0:64, :], onesf[64:65, 0:64], rr[64:65, :],
                                 start=True, stop=True, tile_position=(64, 0)))
                VE.wait_ge(sT.h, sT.n)
                sD.inc(VE.tensor_copy(usb[0:64, :], bank(bb)[0:64, :]))
                VE.drain()
                pidx = 2 * quad + hq // 2
                if hq % 2 == 0:
                    sD.inc(VE.tensor_tensor(Cqc[0:64, pidx, :], bank(4 + hq)[0:64, :],
                                            usb[0:64, :], ALU.mult))
                else:
                    if quad > 0:
                        VE.wait_ge(ccq.h, ccq.n)
                    sD.inc(VE.tensor_tensor(csc[0:64, hq // 2, :], bank(4 + hq)[0:64, :],
                                            usb[0:64, :], ALU.mult))
                    GP.wait_ge(sD.h, sD.n)
                    ccq.inc(GP.dma_start(out=Cqc[64:128, pidx, :], in_=csc[0:64, hq // 2, :]), 16)
                TN.wait_ge(sD.h, sD.n)
        m_attn = sD.n
        TN.wait_ge(sD.h, m_attn)
        TN.wait_ge(ccq.h, ccq.n)
        TN.wait_ge(xcq.h, 128 * xc_cnt[0])
        for c in range(NC_):
            slw = c % 2
            TN.wait_ge(woq[slw].h, 16 * (wo_base[slw] + c // 2 + 1))
            for d in range(8):
                ins = TN.matmul(bank(d)[:, :], wobuf[:, slw, ts(d, 128)], Cqc[:, c, :],
                                start=(c == 0), stop=False, tile_position=(0, 0),
                                skip_group_check=True)
            sT.inc(ins)
            wo_cons[slw] = sT.n
            if c + 2 < NC_:
                wo_load(c + 2)
        for d in range(8):
            sT.inc(TN.matmul(bank(d)[:, :], bor[0:1, ts(d, 128)], onesrow[0:1, :],
                             start=False, stop=True, tile_position=(0, 0), skip_group_check=True))
        m_oproj = sT.n
        VE.wait_ge(sT.h, m_oproj)
        for d in range(8):
            sD.inc(VE.scalar_tensor_tensor(Xqc[:, d, :], bank(d)[:, :], 1.0, Xqc[:, d, :],
                                           ALU.mult, ALU.add))
        SC.wait_ge(stq[0].h, 16 * st_state["cnt"][0])
        SC.wait_ge(stq[1].h, 16 * st_state["cnt"][1])
        srcs = [Xqc[:, c, :] for c in range(NC_)]
        lnm = emit_ln(srcs, Xb3, sq3, LN3)
        emit_ffn(LN3, lnm, srcs, w1buf4, w2buf4, hT4)
        emit_store_chunk(Xqc, qc * QCW, stflat4)
        xgate[0] = sT.n
        xgate[1] = sD.n

    SY.wait_ge(stq[0].h, 16 * st_state["cnt"][0])
    SY.wait_ge(stq[1].h, 16 * st_state["cnt"][1])
    blk_cm.__exit__(None, None, None)
    esP4.close()
    es.close()
    return nc


def kernel(**inputs):
    x = np.asarray(inputs["L0_attn_x"])
    wq = np.asarray(inputs["L0_attn_wq"])
    bq = np.asarray(inputs["L0_attn_bq"])
    wk = np.asarray(inputs["L0_attn_wk"])
    bk = np.asarray(inputs["L0_attn_bk"])
    wv = np.asarray(inputs["L0_attn_wv"])
    bv = np.asarray(inputs["L0_attn_bv"])
    wo = np.asarray(inputs["L0_attn_wo"])
    bo = np.asarray(inputs["L0_attn_bo"])
    relb = np.asarray(inputs["L0_attn_rel_bias"])
    W1 = np.asarray(inputs["L0_ffn_W1"])
    b1 = np.asarray(inputs["L0_ffn_b1"])
    W2 = np.asarray(inputs["L0_ffn_W2"])
    b2 = np.asarray(inputs["L0_ffn_b2"])

    cst = host_constants()
    shared = {
        "wqT": np.ascontiguousarray(wq.T), "wkT": np.ascontiguousarray(wk.T),
        "wvT": np.ascontiguousarray(wv.T), "woT": np.ascontiguousarray(wo.T),
        "W1": np.ascontiguousarray(W1), "W2": np.ascontiguousarray(W2),
        "bq": bq.reshape(1, -1), "bk": bk.reshape(1, -1), "bv": bv.reshape(1, -1),
        "bo": bo.reshape(1, -1), "b1": b1.reshape(1, -1), "b2": b2.reshape(1, -1),
        "relb": np.ascontiguousarray(relb),
        "relb0": np.ascontiguousarray(np.tile(relb[0:1, :], (32, 1))),
        "relbT": np.ascontiguousarray(relb.T),
        "OH": cst["OH"], "OHP": cst["OHP"], "ident": cst["ident"], "bd": cst["bd"],
    }
    in_maps = []
    for i in range(8):
        b_, h_ = i // 2, i % 2
        m = dict(shared)
        m["x"] = np.ascontiguousarray(x[b_, h_ * T:(h_ + 1) * T, :])
        m["par2"] = np.array([[h_, 1 - h_]], np.int32)
        m["parf"] = np.full((32, 1), float(h_), np.float32)
        in_maps.append(m)

    nc = build("full")
    mybir.codegen_inst_isa_subclasses(nc)
    res = run_bass_kernel_spmd(nc, in_maps, core_ids=list(range(8)))
    out = np.empty((4, 4096, 1024), np.float32)
    for i in range(8):
        b_, h_ = i // 2, i % 2
        out[b_, h_ * T:(h_ + 1) * T, :] = res.results[i]["y"]
    return out


if __name__ == "__main__":
    nc = build("full")
    print("instructions:", nc.n_instructions())


# version 10 (final)
# speedup vs baseline: 1.5589x; 1.3484x over previous
import contextlib
import numpy as np
import ml_dtypes
import bass_rust
import concourse.bass as bass
import concourse.mybir as mybir
from concourse.engine_type import EngineType
from concourse.bass_utils import run_bass_kernel_spmd
from concourse.library_config import proxy as proxy_lib

F32 = mybir.dt.float32
BF16 = mybir.dt.bfloat16
I32 = mybir.dt.int32
AF = mybir.ActivationFunctionType
ALU = mybir.AluOpType
AX = mybir.AxisListType
ts = bass.ts

D = 1024
T = 2048
IT = 4096
NC_ = 8
NTC = 4
TCW = 512
NP = 8
NKT = 32
NQC = 4
QCW = 512
EPS = 1e-5


def _bucket_np(e):
    n = np.asarray(e, np.int64)
    is_small = n < 16
    nf = np.maximum(n, 1).astype(np.float32)
    val = (np.log(nf / np.float32(16.0)).astype(np.float32)
           / np.float32(np.log(np.float32(8.0))) * np.float32(16.0))
    vl = 16 + val.astype(np.int32)
    vl = np.minimum(vl, 31)
    ret = np.where(is_small, n, vl)
    return np.maximum(ret, 0).astype(np.int64)


def host_constants():
    c = {}
    tt = np.arange(368)
    c["OH"] = (_bucket_np(239 - tt)[None, :] == np.arange(32)[:, None]).astype(ml_dtypes.bfloat16)
    tt2 = np.arange(256)
    c["OHP"] = (_bucket_np(255 - tt2)[None, :] == np.arange(32)[:, None]).astype(ml_dtypes.bfloat16)
    c["ident"] = np.eye(128, dtype=np.float32)
    bd = np.zeros((8, 128, 16), dtype=ml_dtypes.bfloat16)
    for p in range(8):
        bd[p, 0:64, 2 * p] = 1
        bd[p, 64:128, 2 * p + 1] = 1
    c["bd"] = bd
    return c


class Sem:
    def __init__(self, es, nc, name):
        self.h = es.enter_context(nc.semaphore(name))
        self.n = 0

    def inc(self, instr, k=1):
        instr.then_inc(self.h, k)
        self.n += k
        return instr


def custom_ap(src, pairs, offset):
    a = src
    a.ap = bass_rust.VecI64Pair(pairs)
    a.offset = offset
    return a


def own_regions(j, qc):
    a0, a1 = qc * QCW, qc * QCW + QCW
    wlo, whi = 128 * j, min(128 * j + 240, T)
    out = []
    hi0 = min(a1, whi)
    if hi0 > a0:
        blo, bhi = max(a0, wlo), min(hi0, whi)
        out.append((a0, hi0, 0, blo, max(bhi, blo)))
    lo1 = max(a0, whi)
    if a1 > lo1:
        out.append((lo1, a1, 1, 0, 0))
    return out


def partner_regions(j, qc):
    a0, a1 = qc * QCW, qc * QCW + QCW
    if j == 31 and qc == 0:
        return [(0, 128, 0, 0, 128), (128, QCW, 0, 0, 0)]
    return [(a0, a1, 0, 0, 0)]


def build(stage="full"):
    nc = bass.Bass("TRN2", target_bir_lowering=False)
    es = contextlib.ExitStack()

    dx = nc.dram_tensor("x", [T, D], F32, kind="ExternalInput").ap()
    dwqT = nc.dram_tensor("wqT", [D, D], F32, kind="ExternalInput").ap()
    dwkT = nc.dram_tensor("wkT", [D, D], F32, kind="ExternalInput").ap()
    dwvT = nc.dram_tensor("wvT", [D, D], F32, kind="ExternalInput").ap()
    dwoT = nc.dram_tensor("woT", [D, D], F32, kind="ExternalInput").ap()
    dW1 = nc.dram_tensor("W1", [D, IT], F32, kind="ExternalInput").ap()
    dW2 = nc.dram_tensor("W2", [IT, D], F32, kind="ExternalInput").ap()
    dbq = nc.dram_tensor("bq", [1, D], F32, kind="ExternalInput").ap()
    dbk = nc.dram_tensor("bk", [1, D], F32, kind="ExternalInput").ap()
    dbv = nc.dram_tensor("bv", [1, D], F32, kind="ExternalInput").ap()
    dbo = nc.dram_tensor("bo", [1, D], F32, kind="ExternalInput").ap()
    db1 = nc.dram_tensor("b1", [1, IT], F32, kind="ExternalInput").ap()
    db2 = nc.dram_tensor("b2", [1, D], F32, kind="ExternalInput").ap()
    drelb = nc.dram_tensor("relb", [32, 16], F32, kind="ExternalInput").ap()
    drelb0 = nc.dram_tensor("relb0", [32, 16], F32, kind="ExternalInput").ap()
    drelbT = nc.dram_tensor("relbT", [16, 32], F32, kind="ExternalInput").ap()
    dOH = nc.dram_tensor("OH", [32, 368], BF16, kind="ExternalInput").ap()
    dOHP = nc.dram_tensor("OHP", [32, 256], BF16, kind="ExternalInput").ap()
    dident = nc.dram_tensor("ident", [128, 128], F32, kind="ExternalInput").ap()
    dbd = nc.dram_tensor("bd", [8, 128, 16], BF16, kind="ExternalInput").ap()
    dpar2 = nc.dram_tensor("par2", [1, 2], I32, kind="ExternalInput").ap()
    dparf = nc.dram_tensor("parf", [32, 1], F32, kind="ExternalInput").ap()
    dy = nc.dram_tensor("y", [T, D], F32, kind="ExternalOutput").ap()

    w1bf = nc.dram_tensor("w1bf", [32, 128, 8, 128], BF16, kind="Internal").ap()
    w2bf = nc.dram_tensor("w2bf", [8, 128, 32, 128], BF16, kind="Internal").ap()
    wqbf = nc.dram_tensor("wqbf", [128, 8, 1024], BF16, kind="Internal").ap()
    wkbf = nc.dram_tensor("wkbf", [128, 8, 1024], BF16, kind="Internal").ap()
    wvbf = nc.dram_tensor("wvbf", [128, 8, 1024], BF16, kind="Internal").ap()
    wobf = nc.dram_tensor("wobf", [8, 128, 1024], BF16, kind="Internal").ap()
    qdram = nc.dram_tensor("qdram", [NP, 128, T], BF16, kind="Internal").ap()
    xdram = nc.dram_tensor("xdram", [NC_, 128, T], F32, kind="Internal").ap()
    frevd = nc.dram_tensor("frevd", [16, 368], F32, kind="Internal").ap()
    frevpd = nc.dram_tensor("frevpd", [16, 256], F32, kind="Internal").ap()
    kstage = nc.dram_tensor("kstage", [NP, 128, T], BF16, kind="Internal").ap()
    vstage = nc.dram_tensor("vstage", [NP, 128, 16, 130], BF16, kind="Internal").ap()
    kpar = nc.dram_tensor("kpar", [NP, 128, T], BF16, kind="Internal").ap()
    vpar = nc.dram_tensor("vpar", [NP, 128, 16, 130], BF16, kind="Internal").ap()
    kshk = nc.dram_tensor("kshk", [2, NP, 128, T], BF16, kind="Internal", addr_space="Shared").ap()
    kshv = nc.dram_tensor("kshv", [2, NP, 128, 16, 130], BF16, kind="Internal", addr_space="Shared").ap()
    kshm = nc.dram_tensor("kshm", [2, 16, 1], F32, kind="Internal", addr_space="Shared").ap()

    sT = Sem(es, nc, "sT")
    sA = Sem(es, nc, "sA")
    sD = Sem(es, nc, "sD")
    sG = Sem(es, nc, "sG")
    sSig = Sem(es, nc, "sSig")
    sPrep = Sem(es, nc, "sPrep")
    sPrepL = Sem(es, nc, "sPrepL")
    cvq = Sem(es, nc, "cvq")
    scq = Sem(es, nc, "scq")
    xq = [Sem(es, nc, f"xq{i}") for i in range(2)]
    spq = Sem(es, nc, "spq")
    ftq = Sem(es, nc, "ftq")
    bndq = Sem(es, nc, "bndq")
    shq1 = Sem(es, nc, "shq1")
    shq2 = Sem(es, nc, "shq2")
    pwq = Sem(es, nc, "pwq")
    evq_q = [Sem(es, nc, f"evq{i}") for i in range(4)]
    kvqA = [Sem(es, nc, f"kvqA{i}") for i in range(2)]
    kvqB = [Sem(es, nc, f"kvqB{i}") for i in range(2)]
    xcq = Sem(es, nc, "xcq")
    woq = [Sem(es, nc, f"woq{i}") for i in range(2)]
    ccq = Sem(es, nc, "ccq")
    vfq = Sem(es, nc, "vfq")
    stq = [Sem(es, nc, f"stq{i}") for i in range(2)]
    w1q = [Sem(es, nc, f"w1q{i}") for i in range(2)]
    w2q = [Sem(es, nc, f"w2q{i}") for i in range(2)]
    vsl = [Sem(es, nc, f"vsl{i}") for i in range(2)]
    vss = [Sem(es, nc, f"vss{i}") for i in range(2)]
    vrq = Sem(es, nc, "vrq")

    sbt = lambda name, shape, dt: es.enter_context(nc.sbuf_tensor(name, shape, dt))
    ident = sbt("identS", [128, 128], F32)
    onescol = sbt("onescol", [128, 1], BF16)
    onesf = sbt("onesf", [128, 128], F32)
    onesrow = sbt("onesrow", [1, 512], BF16)
    bqr = sbt("bqr", [1, D], BF16)
    bkr = sbt("bkr", [1, D], BF16)
    bvr = sbt("bvr", [1, D], BF16)
    bor = sbt("bor", [1, D], BF16)
    b1r = sbt("b1r", [1, IT], BF16)
    b2r = sbt("b2r", [1, D], BF16)
    bdsb = sbt("bdsb", [128, 8, 16], BF16)
    ps2 = sbt("ps2", [1, 2], I32)
    bandDr = sbt("bandDr", [128, 16, 240], F32)
    bndDr = sbt("bndDr", [128, 16, 128], F32)
    bcol = sbt("bcol", [128, 192], F32)
    qmax2 = sbt("qmax2", [128, 4], F32)
    kmaxt = sbt("kmaxt", [128, 6], F32)
    kmax2b = sbt("kmax2b", [128, 1], F32)
    lnrow = sbt("lnrow", [1, 7, 512], F32)
    rstdb = sbt("rstdb", [128, 512], BF16)
    mrsb = sbt("mrsb", [128, 512], BF16)
    relsb = sbt("relsb", [32, 64], F32)
    relbpb = sbt("relbpb", [32, 32], BF16)
    relT = sbt("relT", [16, 40], F32)
    Bt = sbt("Bt", [128, 16], F32)
    bcpm = sbt("bcpm", [16, 12], F32)
    bcrow = sbt("bcrow", [1, 192], F32)
    parfsb = sbt("parfsb", [32, 1], F32)
    vscrow = sbt("vscrow", [1, 16], F32)
    vsc16 = sbt("vsc16", [128, 16], F32)

    pa = es.enter_context(nc.psum_tensor("pa", [128, 8, 512], F32))

    def bank(i):
        return pa[:, i, :]

    blk_cm = nc.Block()
    blk = blk_cm.__enter__()
    TN, VE, SC, GP, SY = nc.tensor, nc.vector, nc.scalar, nc.gpsimd, nc.sync

    GP.load_library(proxy_lib)

    def emit_passthru():
        esmin = contextlib.ExitStack()
        xs0 = esmin.enter_context(nc.sbuf_tensor("xs0", [128, 2, D], F32))
        for tt in range(16):
            sl = tt % 2
            if tt >= 2:
                SY.wait_ge(stq[sl].h, 16 * (tt // 2))
            xq[sl].inc(SY.dma_start(out=xs0[:, sl, :], in_=dx[ts(tt, 128), :]), 16)
            SY.wait_ge(xq[sl].h, xq[sl].n)
            stq[sl].inc(SY.dma_start(out=dy[ts(tt, 128), :], in_=xs0[:, sl, :]), 16)
        SY.wait_ge(stq[0].h, 16 * 8)
        SY.wait_ge(stq[1].h, 16 * 8)
        blk_cm.__exit__(None, None, None)
        esmin.close()
        es.close()
        return nc

    if stage == "pt":
        return emit_passthru()
    for ii in range(32):
        cvq.inc(GP.dma_start(out=w1bf[ii, :, :, :],
                             in_=custom_ap(dW1[:, :], [(IT, 128), (128 * IT, 8), (1, 128)],
                                           ii * 128)), 16)
    for dd in range(8):
        for i4 in range(4):
            cvq.inc(GP.dma_start(out=w2bf[dd, :, 8 * i4:8 * i4 + 8, :],
                                 in_=custom_ap(dW2[:, :], [(D, 128), (128 * D, 8), (1, 128)],
                                               dd * 128 + 8 * i4 * 128 * D)), 16)
    for (wdst, wsrc) in ((wqbf, dwqT), (wkbf, dwkT), (wvbf, dwvT)):
        cvq.inc(GP.dma_start(out=wdst[:, :, :],
                             in_=custom_ap(wsrc[:, :], [(D, 128), (128 * D, 8), (1, 1024)], 0)), 16)
    cvq.inc(GP.dma_start(out=wobf[:, :, :], in_=dwoT.rearrange("(c p) d -> c p d", p=128)), 16)
    for (dst, src) in ((bqr, dbq), (bkr, dbk), (bvr, dbv), (bor, dbo), (b1r, db1), (b2r, db2)):
        cvq.inc(GP.dma_start(out=dst[:, :], in_=src[:, :]), 16)
    cvq.inc(GP.dma_start(out=bdsb[:, :, :], in_=dbd.rearrange("p r c -> r p c")), 16)
    cvq.inc(GP.dma_start(out=relsb[:, 0:16], in_=drelb[:, :]), 16)
    cvq.inc(GP.dma_start(out=relsb[:, 16:32], in_=drelb0[:, :]), 16)
    cvq.inc(GP.dma_start(out=relT[:, 0:32], in_=drelbT[:, :]), 16)
    m_cvq = cvq.n

    scq.inc(SY.dma_start(out=ident[:, :], in_=dident[:, :]), 16)
    scq.inc(SY.dma_start(out=ps2[:, :], in_=dpar2[:, :]), 16)
    scq.inc(SY.dma_start(out=parfsb[:, :], in_=dparf[:, :]), 16)
    m_scq = scq.n

    sG.inc(GP.memset(onescol[:, :], 1.0))
    sG.inc(GP.memset(onesf[:, :], 1.0))
    sG.inc(GP.memset(onesrow[:, :], 1.0))
    m_memset = sG.n
    esOB = contextlib.ExitStack()
    onesbig = esOB.enter_context(nc.sbuf_tensor("onesbig", [128, 2080], BF16))
    sG.inc(GP.memset(onesbig[:, :], 1.0))
    GP.wait_ge(sG.h, sG.n)
    for p in range(NP):
        vfq.inc(GP.dma_start(out=vstage[p, :, :, :], in_=onesbig[:, :].rearrange("p (a b) -> p a b", a=16)), 16)
    m_vfill = vfq.n
    esOB.close()

    if stage == "ptc":
        GP.wait_ge(cvq.h, m_cvq)
        return emit_passthru()

    esX = contextlib.ExitStack()
    X = esX.enter_context(nc.sbuf_tensor("X", [128, NC_, T], F32))
    esXs = contextlib.ExitStack()
    xstage = esXs.enter_context(nc.sbuf_tensor("xstage", [128, 2, D], F32))

    TN.wait_ge(scq.h, m_scq)
    VE.wait_ge(sG.h, m_memset)
    VE.wait_ge(vfq.h, m_vfill)
    SY.wait_ge(vfq.h, m_vfill)
    xtp_marks = []
    xev_marks = []
    for tt in range(16):
        sl = tt % 2
        if tt >= 2:
            SY.wait_ge(sT.h, xtp_marks[tt - 2])
        xq[sl].inc(SY.dma_start(out=xstage[:, sl, :], in_=dx[ts(tt, 128), :]), 16)
        TN.wait_ge(xq[sl].h, xq[sl].n)
        for cc in range(NC_):
            step = tt * NC_ + cc
            b = step % 4
            if step >= 4:
                TN.wait_ge(sD.h, xev_marks[step - 4])
            sT.inc(TN.transpose(bank(b)[:, 0:128], xstage[:, sl, ts(cc, 128)], ident[:, :]))
            VE.wait_ge(sT.h, sT.n)
            sD.inc(VE.tensor_copy(X[:, cc, ts(tt, 128)], bank(b)[:, 0:128]))
            xev_marks.append(sD.n)
        xtp_marks.append(sT.n)
    esXs.close()

    def emit_ln(src_slices, Xb, sq, LNout):
        VE.drain()
        for c in range(NC_):
            sD.inc(VE.tensor_copy(Xb[:, c, :], src_slices[c]))
        VE.drain()
        for c in range(NC_):
            sD.inc(VE.tensor_tensor(sq[:, c, :], Xb[:, c, :], Xb[:, c, :], ALU.mult))
        TN.wait_ge(sD.h, sD.n)
        for c in range(NC_):
            sT.inc(TN.matmul(bank(4)[0:1, :], onescol[:, 0:1], Xb[:, c, :],
                             start=(c == 0), stop=(c == NC_ - 1), tile_position=(0, 0),
                             skip_group_check=True))
        for c in range(NC_):
            sT.inc(TN.matmul(bank(5)[0:1, :], onescol[:, 0:1], sq[:, c, :],
                             start=(c == 0), stop=(c == NC_ - 1), tile_position=(0, 0),
                             skip_group_check=True))
        VE.wait_ge(sT.h, sT.n)
        VE.tensor_scalar(lnrow[0:1, 0, :], bank(4)[0:1, :], 1.0 / D, None, ALU.mult)
        VE.tensor_scalar(lnrow[0:1, 1, :], bank(5)[0:1, :], 1.0 / D, None, ALU.mult)
        VE.drain()
        VE.tensor_tensor(lnrow[0:1, 2, :], lnrow[0:1, 0, :], lnrow[0:1, 0, :], ALU.mult)
        VE.drain()
        VE.tensor_tensor(lnrow[0:1, 3, :], lnrow[0:1, 1, :], lnrow[0:1, 2, :], ALU.subtract)
        VE.drain()
        sD.inc(VE.tensor_scalar(lnrow[0:1, 3, :], lnrow[0:1, 3, :], EPS, None, ALU.add))
        SC.wait_ge(sD.h, sD.n)
        SC.activation(lnrow[0:1, 6, :], lnrow[0:1, 3, :], AF.Ln)
        SC.drain()
        sA.inc(SC.activation(lnrow[0:1, 4, :], lnrow[0:1, 6, :], AF.Exp, scale=-0.5))
        VE.wait_ge(sA.h, sA.n)
        VE.drain()
        sD.inc(VE.tensor_tensor(lnrow[0:1, 5, :], lnrow[0:1, 0, :], lnrow[0:1, 4, :], ALU.mult))
        TN.wait_ge(sA.h, sA.n)
        sT.inc(TN.matmul(bank(6)[:, :], onesf[0:1, 0:128], lnrow[0:1, 4, :],
                         start=True, stop=True, tile_position=(0, 0)))
        VE.wait_ge(sT.h, sT.n)
        sD.inc(VE.tensor_copy(rstdb[:, :], bank(6)[:, :]))
        TN.wait_ge(sD.h, sD.n)
        sT.inc(TN.matmul(bank(6)[:, :], onesf[0:1, 0:128], lnrow[0:1, 5, :],
                         start=True, stop=True, tile_position=(0, 0)))
        VE.wait_ge(sT.h, sT.n)
        sD.inc(VE.tensor_copy(mrsb[:, :], bank(6)[:, :]))
        VE.drain()
        for c in range(NC_):
            VE.tensor_tensor(sq[:, c, :], Xb[:, c, :], rstdb[:, :], ALU.mult)
        VE.drain()
        for c in range(NC_):
            sD.inc(VE.tensor_tensor(LNout[:, c, :], sq[:, c, :], mrsb[:, :], ALU.subtract))
        return sD.n

    fst = {"w1g": 0, "w2g": 0, "h": [0, 0], "hlast": 0,
           "w1cons": [0, 0], "w2cons": [0, 0]}

    def w1_load(gi, w1buf):
        g = gi % 16 if False else None
        slq = w1q[gi % 2]
        GP.wait_ge(sT.h, fst["w1cons"][gi % 2])
        gg = gi - (gi // 16) * 16
        for iw in range(2):
            slq.inc(GP.dma_start(out=w1buf[:, gi % 2, iw, :, :],
                                 in_=w1bf[2 * gg + iw, :, :, :]), 16)

    def w2_load(di, w2buf):
        slq = w2q[di % 2]
        GP.wait_ge(sT.h, fst["w2cons"][di % 2])
        hs = di - (di // 16) * 16
        d, hf = hs // 2, hs % 2
        for iw in range(2):
            slq.inc(GP.dma_start(out=w2buf[:, di % 2, 8 * iw:8 * iw + 8, :],
                                 in_=w2bf[d, :, 16 * hf + 8 * iw:16 * hf + 8 * iw + 8, :]), 16)

    def emit_ffn(LNout, ln_mark, Xdst, w1buf, w2buf, hT):
        gbase = fst["w1g"]
        w1_load(gbase + 0, w1buf)
        w1_load(gbase + 1, w1buf)
        TN.wait_ge(sD.h, ln_mark)
        for i in range(32):
            gi = gbase + i // 2
            if i % 2 == 0:
                TN.wait_ge(w1q[gi % 2].h, 32 * (gi // 2 + 1))
            b = i % 2
            TN.wait_ge(sA.h, fst["h"][b])
            for c in range(NC_):
                TN.matmul(bank(b)[:, :], w1buf[:, gi % 2, i % 2, c, :], LNout[:, c, :],
                          start=(c == 0), stop=False, tile_position=(0, 0), skip_group_check=True)
            sT.inc(TN.matmul(bank(b)[:, :], b1r[0:1, ts(i, 128)], onesrow[0:1, :],
                             start=False, stop=True, tile_position=(0, 0), skip_group_check=True))
            if i % 2 == 1:
                fst["w1cons"][gi % 2] = sT.n
                if gi + 2 < gbase + 16:
                    w1_load(gi + 2, w1buf)
            SC.wait_ge(sT.h, sT.n)
            sA.inc(SC.activation(hT[:, i, :], bank(b)[:, :], AF.Relu))
            fst["h"][b] = sA.n
        fst["w1g"] += 16
        fst["hlast"] = sA.n
        dbase = fst["w2g"]
        w2_load(dbase + 0, w2buf)
        w2_load(dbase + 1, w2buf)
        ylast = [0, 0]
        for d in range(8):
            TN.wait_ge(sA.h, fst["hlast"])
            b = 2 + d % 2
            TN.wait_ge(sD.h, ylast[d % 2])
            for i in range(32):
                di = dbase + 2 * d + i // 16
                if i % 16 == 0:
                    TN.wait_ge(w2q[di % 2].h, 32 * (di // 2 + 1))
                TN.matmul(bank(b)[:, :], w2buf[:, di % 2, i % 16, :], hT[:, i, :],
                          start=(i == 0), stop=False, tile_position=(0, 0), skip_group_check=True)
                if i % 16 == 15:
                    fst["w2cons"][di % 2] = sT.n + 1
            sT.inc(TN.matmul(bank(b)[:, :], b2r[0:1, ts(d, 128)], onesrow[0:1, :],
                             start=False, stop=True, tile_position=(0, 0), skip_group_check=True))
            fst["w2cons"][(dbase + 2 * d) % 2] = sT.n
            fst["w2cons"][(dbase + 2 * d + 1) % 2] = sT.n
            for nx in (dbase + 2 * d + 2, dbase + 2 * d + 3):
                if nx < dbase + 16:
                    w2_load(nx, w2buf)
            VE.wait_ge(sT.h, sT.n)
            sD.inc(VE.scalar_tensor_tensor(Xdst[d], bank(b)[:, :], 0.5, Xdst[d], ALU.mult, ALU.add))
            ylast[d % 2] = sD.n
        fst["w2g"] += 16
        return sD.n

    st_state = {"n": 0, "marks": {}, "cnt": [0, 0]}

    def emit_store_chunk(Xsrc, tok0, stflat):
        TN.wait_ge(sD.h, sD.n)
        for tt in range(4):
            sl = tt % 2
            for cc in range(NC_):
                step = st_state["n"]
                st_state["n"] += 1
                b = step % 4
                if step >= 4:
                    TN.wait_ge(sD.h, st_state["marks"][step - 4])
                sT.inc(TN.transpose(bank(b)[:, 0:128], Xsrc[:, cc, ts(tt, 128)], ident[:, :]))
                VE.wait_ge(sT.h, sT.n)
                if cc == 0 and st_state["cnt"][sl] > 0:
                    VE.wait_ge(stq[sl].h, 16 * st_state["cnt"][sl])
                sD.inc(VE.tensor_copy(stflat[:, sl * 1024 + cc * 128: sl * 1024 + cc * 128 + 128],
                                      bank(b)[:, 0:128]))
                st_state["marks"][step] = sD.n
            SY.wait_ge(sD.h, sD.n)
            stq[sl].inc(SY.dma_start(out=dy[tok0 + tt * 128: tok0 + (tt + 1) * 128, :],
                                     in_=stflat[:, sl * 1024: sl * 1024 + 1024]), 16)
            st_state["cnt"][sl] += 1

    VE.wait_ge(cvq.h, m_cvq)
    VE.wait_ge(scq.h, m_scq)
    VE.tensor_tensor(relsb[:, 32:48], relsb[:, 0:16], relsb[:, 16:32], ALU.subtract)
    VE.drain()
    VE.tensor_scalar(relsb[:, 48:64], relsb[:, 32:48], parfsb[:, 0:1], None, ALU.mult)
    VE.drain()
    VE.tensor_copy(relbpb[:, 0:16], relsb[:, 32:48])
    sD.inc(VE.tensor_copy(relbpb[:, 16:32], relsb[:, 48:64]))
    VE.tensor_scalar(relT[:, 0:32], relT[:, 0:32], relT[:, 0:1], None, ALU.subtract)
    VE.drain()
    sD.inc(VE.reduce_max(relT[:, 32:33], relT[:, 0:32], axis=AX.X, apply_absolute_value=True))
    VE.drain()
    sD.inc(VE.tensor_scalar(relT[:, 34:35], relT[:, 31:32], parfsb[0:16, 0:1], None, ALU.mult))
    m_rel = sD.n

    esOH = contextlib.ExitStack()
    oh_s = esOH.enter_context(nc.sbuf_tensor("oh_s", [32, 368], BF16))
    ohp_s = esOH.enter_context(nc.sbuf_tensor("ohp_s", [32, 256], BF16))
    frow = esOH.enter_context(nc.sbuf_tensor("frow", [16, 368], F32))
    GP.wait_ge(sT.h, sT.n)
    GP.wait_ge(sD.h, sD.n)
    cvq.inc(GP.dma_start(out=oh_s[:, :], in_=dOHP[:, :] if False else dOH[:, :]), 16)
    cvq.inc(GP.dma_start(out=ohp_s[:, :], in_=dOHP[:, :]), 16)
    TN.wait_ge(cvq.h, cvq.n)
    TN.wait_ge(sD.h, sD.n)
    sT.inc(TN.matmul(bank(0)[0:16, 0:368], relbpb[:, 0:16], oh_s[:, :],
                     start=True, stop=True, tile_position=(0, 0)))
    VE.wait_ge(sT.h, sT.n)
    sD.inc(VE.tensor_copy(frow[:, :], bank(0)[0:16, 0:368]))
    SY.wait_ge(sD.h, sD.n)
    ftq.inc(SY.dma_start(out=frevd[:, :], in_=frow[:, :]), 16)
    TN.wait_ge(sD.h, sD.n)
    sT.inc(TN.matmul(bank(1)[0:16, 0:256], relbpb[:, 16:32], ohp_s[:, :],
                     start=True, stop=True, tile_position=(0, 0)))
    VE.wait_ge(sT.h, sT.n)
    VE.wait_ge(ftq.h, ftq.n)
    VE.tensor_copy(frow[:, 0:256], bank(1)[0:16, 0:256])
    VE.drain()
    sD.inc(VE.tensor_scalar(frow[:, 0:256], frow[:, 0:256], relT[:, 34:35], None, ALU.subtract))
    SY.wait_ge(sD.h, sD.n)
    ftq.inc(SY.dma_start(out=frevpd[:, :], in_=frow[:, 0:256]), 16)
    GP.wait_ge(ftq.h, ftq.n)
    for gh in range(16):
        bndq.inc(GP.dma_start(out=bandDr[:, gh, :],
                              in_=custom_ap(frevd[0:1, :], [(1, 128), (1, 240)], gh * 368)), 16)
        bndq.inc(GP.dma_start(out=bndDr[:, gh, :],
                              in_=custom_ap(frevpd[0:1, :], [(1, 128), (1, 128)], gh * 256)), 16)
    m_bndq = bndq.n
    esOH.close()
    VE.wait_ge(ftq.h, ftq.n)
    GP.wait_ge(ftq.h, ftq.n)

    esP1 = contextlib.ExitStack()
    Xb1 = esP1.enter_context(nc.sbuf_tensor("Xb1", [128, NC_, TCW], BF16))
    sq1 = esP1.enter_context(nc.sbuf_tensor("sq1", [128, NC_, TCW], BF16))
    LN1 = esP1.enter_context(nc.sbuf_tensor("LN1", [128, NC_, TCW], BF16))
    w1buf = esP1.enter_context(nc.sbuf_tensor("w1buf", [128, 2, 2, 8, 128], BF16))
    w2buf = esP1.enter_context(nc.sbuf_tensor("w2buf", [128, 2, 16, 128], BF16))
    hT = esP1.enter_context(nc.sbuf_tensor("hT", [128, 32, TCW], BF16))

    TN.wait_ge(cvq.h, m_cvq)
    TN.wait_ge(sG.h, m_memset)
    VE.wait_ge(sG.h, m_memset)
    SC.wait_ge(sG.h, m_memset)
    GP.wait_ge(cvq.h, m_cvq)
    if stage != "xonly":
        for tc in range(NTC):
            srcs = [X[:, c, ts(tc, TCW)] for c in range(NC_)]
            lnm = emit_ln(srcs, Xb1, sq1, LN1)
            emit_ffn(LN1, lnm, srcs, w1buf, w2buf, hT)

    if stage in ("ffn1", "xonly"):
        stflat = hT[:, 0:8, :].bitcast(F32).rearrange("p a b -> p (a b)")
        for tc in range(NTC):
            emit_store_chunk(X[:, :, ts(tc, TCW)], tc * TCW, stflat)
        SY.wait_ge(stq[0].h, 16 * st_state["cnt"][0])
        SY.wait_ge(stq[1].h, 16 * st_state["cnt"][1])
        blk_cm.__exit__(None, None, None)
        esP1.close()
        esX.close()
        es.close()
        return nc
    esP1.close()

    esP2 = contextlib.ExitStack()
    wq_s = esP2.enter_context(nc.sbuf_tensor("wq_s", [128, NC_, D], BF16))
    wk_s = esP2.enter_context(nc.sbuf_tensor("wk_s", [128, NC_, D], BF16))
    wv_s = esP2.enter_context(nc.sbuf_tensor("wv_s", [128, NC_, D], BF16))
    Xb2 = esP2.enter_context(nc.sbuf_tensor("Xb2", [128, NC_, TCW], BF16))
    sq2 = esP2.enter_context(nc.sbuf_tensor("sq2", [128, NC_, TCW], BF16))
    LN2 = esP2.enter_context(nc.sbuf_tensor("LN2", [128, NC_, TCW], BF16))
    evq = esP2.enter_context(nc.sbuf_tensor("evq", [128, 4, TCW], BF16))
    sqq = esP2.enter_context(nc.sbuf_tensor("sqq", [128, 2, TCW], BF16))

    GP.wait_ge(vfq.h, m_vfill)
    SY.wait_ge(sT.h, sT.n)
    SY.wait_ge(sA.h, sA.n)
    SY.wait_ge(sD.h, sD.n)
    SY.wait_ge(cvq.h, m_cvq)
    pwq.inc(SY.dma_start(out=wq_s[:, :, :], in_=wqbf[:, :, :]), 16)
    pwq.inc(SY.dma_start(out=wk_s[:, :, :], in_=wkbf[:, :, :]), 16)
    pwq.inc(SY.dma_start(out=wv_s[:, :, :], in_=wvbf[:, :, :]), 16)
    m_pwq = pwq.n
    GP.wait_ge(scq.h, m_scq)

    ev_state = [0] * 4
    evi = 0

    def evq_slot():
        nonlocal evi
        eb = evi % 4
        evi += 1
        if ev_state[eb] > 0:
            VE.wait_ge(evq_q[eb].h, 16 * ev_state[eb])
        return eb

    bank_ev = {0: 0, 1: 0, 2: 0, 3: 0}
    for tc in range(NTC):
        srcs = [X[:, c, ts(tc, TCW)] for c in range(NC_)]
        lnm = emit_ln(srcs, Xb2, sq2, LN2)
        TN.wait_ge(sD.h, lnm)
        if tc == 0:
            TN.wait_ge(pwq.h, m_pwq)
        for which in ("q", "k"):
            wsb = wq_s if which == "q" else wk_s
            brow = bqr if which == "q" else bkr
            tp = (0, 0) if which == "q" else (0, 32)
            dstq = bank(7)[0:16, :] if which == "q" else bank(7)[32:48, :]
            mm = {}
            sq_marks = {}
            st_marks = {}
            for p in range(NP):
                b = p % 2
                if bank_ev[b]:
                    TN.wait_ge(sD.h, bank_ev[b])
                for c in range(NC_):
                    TN.matmul(bank(b)[:, :], wsb[:, c, ts(p, 128)], LN2[:, c, :],
                              start=(c == 0), stop=False, tile_position=(0, 0), skip_group_check=True)
                sT.inc(TN.matmul(bank(b)[:, :], brow[0:1, ts(p, 128)], onesrow[0:1, :],
                                 start=False, stop=True, tile_position=(0, 0), skip_group_check=True))
                mm[p] = sT.n
                if p >= 1:
                    TN.wait_ge(sD.h, sq_marks[p - 1])
                    sT.inc(TN.matmul(dstq, bdsb[:, p - 1, :], sqq[:, (p - 1) % 2, :],
                                     start=(p - 1 == 0), stop=False, tile_position=tp,
                                     skip_group_check=True))
                    st_marks[p - 1] = sT.n
                eb = evq_slot()
                VE.wait_ge(sT.h, mm[p])
                if which == "q":
                    sD.inc(VE.tensor_scalar(evq[:, eb, :], bank(b)[:, :], 0.125, None, ALU.mult))
                else:
                    sD.inc(VE.tensor_copy(evq[:, eb, :], bank(b)[:, :]))
                bank_ev[b] = sD.n
                if p >= 2:
                    VE.wait_ge(sT.h, st_marks[p - 2])
                VE.drain()
                sD.inc(VE.tensor_tensor(sqq[:, p % 2, :], evq[:, eb, :], evq[:, eb, :], ALU.mult))
                sq_marks[p] = sD.n
                GP.wait_ge(sD.h, bank_ev[b])
                if which == "q":
                    evq_q[eb].inc(GP.dma_start(out=qdram[p, :, ts(tc, TCW)], in_=evq[:, eb, :]), 16)
                else:
                    evq_q[eb].inc(GP.dma_start(out=kstage[p, :, ts(tc, TCW)], in_=evq[:, eb, :]), 16)
                ev_state[eb] += 1
            TN.wait_ge(sD.h, sq_marks[NP - 1])
            sT.inc(TN.matmul(dstq, bdsb[:, NP - 1, :], sqq[:, (NP - 1) % 2, :],
                             start=False, stop=True, tile_position=tp, skip_group_check=True))
            VE.wait_ge(sT.h, sT.n)
            if which == "q":
                sD.inc(VE.reduce_max(qmax2[0:16, tc:tc + 1], bank(7)[0:16, :], axis=AX.X))
            else:
                sD.inc(VE.reduce_max(kmaxt[32:48, tc:tc + 1], bank(7)[32:48, :], axis=AX.X))
            TN.wait_ge(sD.h, sD.n)
        for tt in range(4):
            gt = tc * 4 + tt
            for half in range(2):
                b = 2 + (tt * 2 + half) % 2
                if bank_ev[b]:
                    TN.wait_ge(sD.h, bank_ev[b])
                for c in range(NC_):
                    TN.matmul(bank(b)[:, :], LN2[:, c, ts(tt, 128)], wv_s[:, c, ts(half, 512)],
                              start=(c == 0), stop=False, tile_position=(0, 0), skip_group_check=True)
                sT.inc(TN.matmul(bank(b)[:, :], onesrow[0:1, 0:128], bvr[0:1, ts(half, 512)],
                                 start=False, stop=True, tile_position=(0, 0), skip_group_check=True))
                mmv = sT.n
                eb = evq_slot()
                VE.wait_ge(sT.h, mmv)
                sD.inc(VE.tensor_copy(evq[:, eb, :], bank(b)[:, :]))
                bank_ev[b] = sD.n
                GP.wait_ge(sD.h, bank_ev[b])
                for h2 in (0, 1):
                    vsrc = evq[:, eb, :].rearrange("p (pp r) -> p pp r", pp=4)[:, :, 64 * h2:64 * h2 + 64]
                    dstv = custom_ap(vstage[0, :, 0, :],
                                     [(2080, 128), (266240, 4), (1, 64)],
                                     4 * half * 266240 + gt * 130 + 65 * h2)
                    evq_q[eb].inc(GP.dma_start(out=dstv, in_=vsrc), 16)
                ev_state[eb] += 2
    m_evq_fin = [16 * c for c in ev_state]
    esP2.close()

    VE.tensor_tensor(kmaxt[32:48, 4:5], kmaxt[32:48, 0:1], kmaxt[32:48, 1:2], ALU.max)
    VE.drain()
    VE.tensor_tensor(kmaxt[32:48, 4:5], kmaxt[32:48, 4:5], kmaxt[32:48, 2:3], ALU.max)
    VE.drain()
    sD.inc(VE.tensor_tensor(kmaxt[32:48, 4:5], kmaxt[32:48, 4:5], kmaxt[32:48, 3:4], ALU.max))
    GP.wait_ge(sD.h, sD.n)
    for i in range(4):
        GP.wait_ge(evq_q[i].h, m_evq_fin[i])
    isodd = nc.values_load(ps2[0:1, 0:1], engines=[EngineType.Pool], min_val=0, max_val=1)
    iseven = nc.values_load(ps2[0:1, 1:2], engines=[EngineType.Pool], min_val=0, max_val=1)
    shq1.inc(GP.dma_start(out=kshk[0, :, :, :], in_=kstage[:, :, :], cond=iseven), 16)
    shq1.inc(GP.dma_start(out=kshk[1, :, :, :], in_=kstage[:, :, :], cond=isodd), 16)
    shq1.inc(GP.dma_start(out=kshv[0, :, :, :, :], in_=vstage[:, :, :, :], cond=iseven), 16)
    shq1.inc(GP.dma_start(out=kshv[1, :, :, :, :], in_=vstage[:, :, :, :], cond=isodd), 16)
    shq1.inc(GP.dma_start(out=kshm[0, :, :], in_=kmaxt[32:48, 4:5], cond=iseven), 16)
    shq1.inc(GP.dma_start(out=kshm[1, :, :], in_=kmaxt[32:48, 4:5], cond=isodd), 16)
    GP.wait_ge(shq1.h, shq1.n)
    sPrep.inc(GP.remote_sem_update_broadcast(sSig.h, sPrepL.h, rdests=[(0, 1)] + [None] * 7))
    GP.wait_ge(sPrep.h, sPrep.n)
    GP.trigger_dma(count=1)

    GP.wait_ge(sSig.h, 2)
    isodd2 = nc.values_load(ps2[0:1, 0:1], engines=[EngineType.Pool], min_val=0, max_val=1)
    iseven2 = nc.values_load(ps2[0:1, 1:2], engines=[EngineType.Pool], min_val=0, max_val=1)
    shq2.inc(GP.dma_start(out=kpar[:, :, :], in_=kshk[1, :, :, :], cond=iseven2), 16)
    shq2.inc(GP.dma_start(out=kpar[:, :, :], in_=kshk[0, :, :, :], cond=isodd2), 16)
    shq2.inc(GP.dma_start(out=vpar[:, :, :, :], in_=kshv[1, :, :, :, :], cond=iseven2), 16)
    shq2.inc(GP.dma_start(out=vpar[:, :, :, :], in_=kshv[0, :, :, :, :], cond=isodd2), 16)
    shq2.inc(GP.dma_start(out=kmaxt[32:48, 5:6], in_=kshm[1, :, :], cond=iseven2), 16)
    shq2.inc(GP.dma_start(out=kmaxt[32:48, 5:6], in_=kshm[0, :, :], cond=isodd2), 16)
    m_shq2 = shq2.n

    VE.wait_ge(shq2.h, m_shq2)
    sD.inc(VE.tensor_tensor(kmaxt[32:48, 5:6], kmaxt[32:48, 5:6], kmaxt[32:48, 4:5], ALU.max))
    GP.wait_ge(sD.h, sD.n)
    cvq.inc(GP.dma_start(out=kmax2b[0:16, 0:1], in_=kmaxt[32:48, 5:6]), 16)
    VE.wait_ge(cvq.h, cvq.n)
    sD.inc(VE.tensor_scalar(Bt[0:16, 0:4], qmax2[0:16, 0:4], kmax2b[0:16, 0:1], None, ALU.mult))
    SC.wait_ge(sD.h, sD.n)
    SC.activation(Bt[0:16, 4:8], Bt[0:16, 0:4], AF.Ln)
    SC.drain()
    sA.inc(SC.activation(Bt[0:16, 4:8], Bt[0:16, 4:8], AF.Exp, scale=0.5))
    VE.wait_ge(sA.h, sA.n)
    sD.inc(VE.tensor_scalar(Bt[0:16, 4:8], Bt[0:16, 4:8], relT[:, 32:33], None, ALU.add))
    GP.wait_ge(sD.h, sD.n)
    cvq.inc(GP.dma_start(out=bcrow[0:1, 0:64], in_=Bt[0:16, 4:8]), 16)
    VE.wait_ge(cvq.h, cvq.n)
    VE.drain()
    VE.tensor_tensor(custom_ap(bcrow[0:1, :], [(192, 1), (8, 8), (1, 4)], 64),
                     custom_ap(bcrow[0:1, :], [(192, 1), (8, 8), (1, 4)], 0),
                     custom_ap(bcrow[0:1, :], [(192, 1), (8, 8), (1, 4)], 4), ALU.max)
    VE.drain()
    sD.inc(VE.tensor_copy(custom_ap(bcrow[0:1, :], [(192, 1), (8, 8), (1, 4)], 68),
                          custom_ap(bcrow[0:1, :], [(192, 1), (8, 8), (1, 4)], 64)))
    GP.wait_ge(sD.h, sD.n)
    cvq.inc(GP.dma_start(out=Bt[0:16, 4:8], in_=bcrow[0:1, 64:128]), 16)
    VE.wait_ge(cvq.h, cvq.n)
    VE.drain()
    VE.tensor_scalar(Bt[0:16, 8:12], Bt[0:16, 4:8], -1.0, None, ALU.mult)
    VE.drain()
    VE.tensor_copy(bcpm[:, 0:4], Bt[0:16, 8:12])
    VE.tensor_scalar(bcpm[:, 4:8], Bt[0:16, 8:12], relT[:, 31:32], None, ALU.add)
    sD.inc(VE.tensor_scalar(bcpm[:, 8:12], Bt[0:16, 8:12], relT[:, 34:35], None, ALU.add))
    GP.wait_ge(sD.h, sD.n)
    cvq.inc(GP.dma_start(out=bcrow[0:1, 0:192], in_=bcpm[:, :]), 16)
    TN.wait_ge(cvq.h, cvq.n)
    sT.inc(TN.matmul(bank(2)[:, 0:192], onesf[0:1, 0:128], bcrow[0:1, :],
                     start=True, stop=True, tile_position=(0, 0)))
    VE.wait_ge(sT.h, sT.n)
    sD.inc(VE.tensor_copy(bcol[:, :], bank(2)[:, 0:192]))
    m_bcol = sD.n

    SY.wait_ge(sD.h, sD.n)
    for c in range(NC_):
        spq.inc(SY.dma_start(out=xdram[c, :, :], in_=X[:, c, :]), 16)
    m_spq = spq.n

    esVS = contextlib.ExitStack()
    vstg = esVS.enter_context(nc.sbuf_tensor("vstg", [128, 2, 16, 130], BF16))
    SC.wait_ge(sD.h, m_rel)
    sA.inc(SC.activation(relT[:, 35:36], relT[:, 34:35], AF.Exp))
    GP.wait_ge(sA.h, sA.n)
    vrq.inc(GP.dma_start(out=vscrow[0:1, 0:16], in_=relT[:, 35:36]), 16)
    TN.wait_ge(vrq.h, vrq.n)
    sT.inc(TN.matmul(bank(3)[:, 0:16], onesf[0:1, 0:128], vscrow[0:1, 0:16],
                     start=True, stop=True, tile_position=(0, 0)))
    VE.wait_ge(sT.h, sT.n)
    sD.inc(VE.tensor_copy(vsc16[:, :], bank(3)[:, 0:16]))
    m_vsc = sD.n
    for p in range(NP):
        if p == 0:
            GP.wait_ge(shq2.h, m_shq2)
        if p >= 2:
            GP.wait_ge(vss[p % 2].h, 16 * (p // 2))
        vsl[p % 2].inc(GP.dma_start(out=vstg[:, p % 2, :, :], in_=vpar[p, :, :, :]), 16)
        VE.wait_ge(vsl[p % 2].h, 16 * (p // 2 + 1))
        if p == 0:
            VE.wait_ge(sD.h, m_vsc)
        for h2 in (0, 1):
            sD.inc(VE.tensor_scalar(vstg[:, p % 2, :, 65 * h2:65 * h2 + 65],
                                    vstg[:, p % 2, :, 65 * h2:65 * h2 + 65],
                                    vsc16[:, 2 * p + h2:2 * p + h2 + 1], None, ALU.mult))
        SY.wait_ge(sD.h, sD.n)
        vss[p % 2].inc(SY.dma_start(out=vpar[p, :, :, :], in_=vstg[:, p % 2, :, :]), 16)
    m_vss = [vss[0].n, vss[1].n]
    esVS.close()
    esX.close()

    esP4 = contextlib.ExitStack()
    Kbuf = esP4.enter_context(nc.sbuf_tensor("Kbuf", [128, 2, 2, T], BF16))
    Vbuf = esP4.enter_context(nc.sbuf_tensor("Vbuf", [128, 2, 32, 130], BF16))
    Qbuf = esP4.enter_context(nc.sbuf_tensor("Qbuf", [128, 2, 2, QCW], BF16))
    Pbuf = esP4.enter_context(nc.sbuf_tensor("Pbuf", [128, 2, 2, 2, QCW], BF16))
    Cqc = esP4.enter_context(nc.sbuf_tensor("Cqc", [128, NC_, QCW], BF16))
    Xqc = esP4.enter_context(nc.sbuf_tensor("Xqc", [128, NC_, QCW], F32))
    rr = esP4.enter_context(nc.sbuf_tensor("rr", [128, QCW], F32))
    usb = esP4.enter_context(nc.sbuf_tensor("usb", [128, QCW], F32))
    csc = esP4.enter_context(nc.sbuf_tensor("csc", [128, 1, QCW], BF16))
    wobuf = esP4.enter_context(nc.sbuf_tensor("wobuf", [128, 8, D], BF16))
    LN3 = esP4.enter_context(nc.sbuf_tensor("LN3", [128, NC_, TCW], BF16))
    w1buf4 = esP4.enter_context(nc.sbuf_tensor("w1buf4", [128, 2, 2, 8, 128], BF16))
    w2buf4 = esP4.enter_context(nc.sbuf_tensor("w2buf4", [128, 2, 16, 128], BF16))
    hT4 = esP4.enter_context(nc.sbuf_tensor("hT4", [128, 32, TCW], BF16))
    stflat4 = hT4[:, 0:8, :].bitcast(F32).rearrange("p a b -> p (a b)")
    Xb3 = Cqc
    sq3 = Pbuf[:, :, :, :, :].rearrange("p a b c d -> p (a b c) d")

    cntA = [0, 0]
    cntB = [0, 0]
    xgate = [0, 0]
    pv15_mark = [0, 0]
    pv31_mark = [0, 0]
    qlast = {0: [0, 0], 1: [0, 0]}
    xc_cnt = [0]
    wo_cnt = [0, 0]
    wo_cons = [0, 0]

    def kslice(sl, j):
        if j < 16:
            return Kbuf[:, sl, 0, ts(j, 128)]
        return Kbuf[:, sl, 1, ts(j - 16, 128)]

    first_q = [True]

    def emit_A(nq, sl):
        qc_, quad_ = nq // 4, nq % 4
        p = 2 * quad_ + sl
        if cntA[sl] == 0:
            GP.wait_ge(shq2.h, m_shq2)
            GP.wait_ge(spq.h, m_spq)
            GP.wait_ge(vss[0].h, m_vss[0])
            GP.wait_ge(vss[1].h, m_vss[1])
            if first_q[0]:
                for i in range(4):
                    GP.wait_ge(evq_q[i].h, m_evq_fin[i])
                first_q[0] = False
        else:
            GP.wait_ge(sT.h, pv15_mark[sl])
        if qlast[sl][nq % 2]:
            GP.wait_ge(sT.h, qlast[sl][nq % 2])
        kvqA[sl].inc(GP.dma_start(out=Kbuf[:, sl, 0, :], in_=kstage[p, :, :]), 16)
        kvqA[sl].inc(GP.dma_start(out=Vbuf[:, sl, 0:16, :], in_=vstage[p, :, :, :]), 16)
        kvqA[sl].inc(GP.dma_start(out=Qbuf[:, sl, nq % 2, :], in_=qdram[p, :, ts(qc_, QCW)]), 16)
        cntA[sl] += 1

    def emit_B(nq, sl):
        p = 2 * (nq % 4) + sl
        if cntB[sl] > 0:
            GP.wait_ge(sT.h, pv31_mark[sl])
        kvqB[sl].inc(GP.dma_start(out=Kbuf[:, sl, 1, :], in_=kpar[p, :, :]), 16)
        kvqB[sl].inc(GP.dma_start(out=Vbuf[:, sl, 16:32, :], in_=vpar[p, :, :, :]), 16)
        cntB[sl] += 1

    GP.wait_ge(sT.h, sT.n)
    GP.wait_ge(sD.h, sD.n)
    GP.wait_ge(spq.h, m_spq)
    GP.wait_ge(vss[0].h, m_vss[0])
    GP.wait_ge(vss[1].h, m_vss[1])
    for c in range(NC_):
        woq[0].inc(GP.dma_start(out=wobuf[:, c, :], in_=wobf[c, :, :]), 16)
    m_wo = woq[0].n

    for sl in (0, 1):
        emit_A(0, sl)
    for qc in range(NQC):
        SY.wait_ge(spq.h, m_spq)
        SY.wait_ge(vss[0].h, m_vss[0])
        SY.wait_ge(vss[1].h, m_vss[1])
        if xc_cnt[0] > 0:
            SY.wait_ge(sT.h, xgate[0])
            SY.wait_ge(sD.h, xgate[1])
        else:
            SY.wait_ge(sT.h, sT.n)
            SY.wait_ge(sD.h, sD.n)
        for c in range(NC_):
            xcq.inc(SY.dma_start(out=Xqc[:, c, :], in_=xdram[c, :, ts(qc, QCW)]), 16)
        xc_cnt[0] += 1
        for quad in range(4):
            nq = qc * 4 + quad
            for sl in (0, 1):
                emit_B(nq, sl)
                TN.wait_ge(kvqA[sl].h, 48 * cntA[sl])
            TN.wait_ge(sD.h, sD.n)
            TN.wait_ge(sA.h, sA.n)
            TN.wait_ge(bndq.h, m_bndq)
            exp_mark = {0: 0, 1: 0}
            pv_mark = {0: 0, 1: 0}
            pv_hist = {0: {}, 1: {}}

            def emit_S(j, sl):
                if j > 0:
                    TN.wait_ge(sA.h, exp_mark[sl])
                if j == 16:
                    TN.wait_ge(kvqB[sl].h, 32 * cntB[sl])
                TN.matmul(bank(2 * sl)[:, :], kslice(sl, j)[0:64, :], Qbuf[0:64, sl, nq % 2, :],
                          start=True, stop=True, tile_position=(0, 0))
                sT.inc(TN.matmul(bank(2 * sl + 1)[:, :], kslice(sl, j)[64:128, :],
                                 Qbuf[64:128, sl, nq % 2, :],
                                 start=True, stop=True, tile_position=(64, 0)))
                return sT.n

            def emit_exp(j, sl, mS):
                regs = own_regions(j, qc) if j < 16 else partner_regions(j, qc)
                need_band = any(r[4] > r[3] for r in regs)
                m_band_dve = 0
                if need_band:
                    VE.wait_ge(sT.h, mS)
                    for hh in (0, 1):
                        gh = 4 * quad + 2 * sl + hh
                        for (lo, hi, cls, blo, bhi) in regs:
                            if bhi <= blo:
                                continue
                            c0 = blo - qc * QCW
                            c1 = bhi - qc * QCW
                            if j < 16:
                                wlo = 128 * j
                                i0 = 239 - (blo - wlo)
                                i1 = 239 - (bhi - wlo)
                                bsrc = bandDr[:, gh, i0:(i1 if i1 >= 0 else None):-1]
                            else:
                                bsrc = bndDr[:, gh, 127::-1]
                            sD.inc(VE.tensor_tensor(bank(2 * sl + hh)[:, c0:c1],
                                                    bank(2 * sl + hh)[:, c0:c1], bsrc, ALU.add))
                    m_band_dve = sD.n
                SC.wait_ge(sT.h, mS)
                if need_band:
                    SC.wait_ge(sD.h, m_band_dve)
                if j > 1:
                    SC.wait_ge(sT.h, pv_hist[sl][j - 2])
                if len(regs) == 1 and regs[0][2] == 0 and not need_band:
                    bidx = (4 * quad + 2 * sl) * 12 + qc
                    sA.inc(SC.activation(Pbuf[:, sl, j % 2, :, :],
                                         pa[:, 2 * sl:2 * sl + 2, :],
                                         AF.Exp, bias=bcol[:, bidx:bidx + 1], scale=1.0))
                else:
                    for hh in (0, 1):
                        gh = 4 * quad + 2 * sl + hh
                        for (lo, hi, cls, blo, bhi) in regs:
                            c0 = lo - qc * QCW
                            c1 = hi - qc * QCW
                            bidx = gh * 12 + cls * 4 + qc
                            sA.inc(SC.activation(Pbuf[:, sl, j % 2, hh, c0:c1],
                                                 bank(2 * sl + hh)[:, c0:c1],
                                                 AF.Exp, bias=bcol[:, bidx:bidx + 1], scale=1.0))
                exp_mark[sl] = sA.n

            def emit_PV(j, sl):
                TN.wait_ge(sA.h, exp_mark[sl])
                first = (j == 0)
                last = (j == NKT - 1)
                for hh in (0, 1):
                    hq = 2 * sl + hh
                    ins = TN.matmul(bank(4 + hq)[0:65, :], Vbuf[:, sl, j, 65 * hh:65 * hh + 65],
                                    Pbuf[:, sl, j % 2, hh, :],
                                    start=first, stop=last, tile_position=(0, 0),
                                    skip_group_check=True)
                sT.inc(ins)
                pv_mark[sl] = sT.n
                pv_hist[sl][j] = sT.n

            for sl in (0, 1):
                mS = emit_S(0, sl)
                emit_exp(0, sl, mS)
            for j in range(1, NKT):
                if j == 17 and nq + 1 < 16:
                    for sl in (0, 1):
                        pv15_mark[sl] = pv_hist[sl][15]
                        emit_A(nq + 1, sl)
                for sl in (0, 1):
                    mS = emit_S(j, sl)
                    if j == NKT - 1:
                        qlast[sl][nq % 2] = mS
                    emit_PV(j - 1, sl)
                    emit_exp(j, sl, mS)
            for sl in (0, 1):
                emit_PV(NKT - 1, sl)
                pv31_mark[sl] = pv_hist[sl][NKT - 1]
            m_allpv = sT.n
            VE.wait_ge(sT.h, m_allpv)
            for hq in range(4):
                if hq > 0:
                    VE.wait_ge(sT.h, sT.n)
                sD.inc(VE.reciprocal(rr[64:65, :], bank(4 + hq)[64:65, :]))
                VE.drain()
                bb = hq % 2
                TN.wait_ge(sD.h, sD.n)
                sT.inc(TN.matmul(bank(bb)[0:64, :], onesf[64:65, 0:64], rr[64:65, :],
                                 start=True, stop=True, tile_position=(64, 0)))
                VE.wait_ge(sT.h, sT.n)
                sD.inc(VE.tensor_copy(usb[0:64, :], bank(bb)[0:64, :]))
                VE.drain()
                pidx = 2 * quad + hq // 2
                if hq % 2 == 0:
                    sD.inc(VE.tensor_tensor(Cqc[0:64, pidx, :], bank(4 + hq)[0:64, :],
                                            usb[0:64, :], ALU.mult))
                else:
                    if quad > 0:
                        VE.wait_ge(ccq.h, ccq.n)
                    sD.inc(VE.tensor_tensor(csc[0:64, 0, :], bank(4 + hq)[0:64, :],
                                            usb[0:64, :], ALU.mult))
                    GP.wait_ge(sD.h, sD.n)
                    ccq.inc(GP.dma_start(out=Cqc[64:128, pidx, :], in_=csc[0:64, 0, :]), 16)
                TN.wait_ge(sD.h, sD.n)
        m_attn = sD.n
        TN.wait_ge(sD.h, m_attn)
        TN.wait_ge(ccq.h, ccq.n)
        TN.wait_ge(xcq.h, 128 * xc_cnt[0])
        TN.wait_ge(woq[0].h, m_wo)
        for c in range(NC_):
            for d in range(8):
                ins = TN.matmul(bank(d)[:, :], wobuf[:, c, ts(d, 128)], Cqc[:, c, :],
                                start=(c == 0), stop=False, tile_position=(0, 0),
                                skip_group_check=True)
            sT.inc(ins)
        for d in range(8):
            sT.inc(TN.matmul(bank(d)[:, :], bor[0:1, ts(d, 128)], onesrow[0:1, :],
                             start=False, stop=True, tile_position=(0, 0), skip_group_check=True))
        m_oproj = sT.n
        VE.wait_ge(sT.h, m_oproj)
        for d in range(8):
            sD.inc(VE.scalar_tensor_tensor(Xqc[:, d, :], bank(d)[:, :], 1.0, Xqc[:, d, :],
                                           ALU.mult, ALU.add))
        SC.wait_ge(stq[0].h, 16 * st_state["cnt"][0])
        SC.wait_ge(stq[1].h, 16 * st_state["cnt"][1])
        srcs = [Xqc[:, c, :] for c in range(NC_)]
        lnm = emit_ln(srcs, Xb3, sq3, LN3)
        emit_ffn(LN3, lnm, srcs, w1buf4, w2buf4, hT4)
        emit_store_chunk(Xqc, qc * QCW, stflat4)
        xgate[0] = sT.n
        xgate[1] = sD.n

    SY.wait_ge(stq[0].h, 16 * st_state["cnt"][0])
    SY.wait_ge(stq[1].h, 16 * st_state["cnt"][1])
    blk_cm.__exit__(None, None, None)
    esP4.close()
    es.close()
    return nc


def kernel(**inputs):
    x = np.asarray(inputs["L0_attn_x"])
    wq = np.asarray(inputs["L0_attn_wq"])
    bq = np.asarray(inputs["L0_attn_bq"])
    wk = np.asarray(inputs["L0_attn_wk"])
    bk = np.asarray(inputs["L0_attn_bk"])
    wv = np.asarray(inputs["L0_attn_wv"])
    bv = np.asarray(inputs["L0_attn_bv"])
    wo = np.asarray(inputs["L0_attn_wo"])
    bo = np.asarray(inputs["L0_attn_bo"])
    relb = np.asarray(inputs["L0_attn_rel_bias"])
    W1 = np.asarray(inputs["L0_ffn_W1"])
    b1 = np.asarray(inputs["L0_ffn_b1"])
    W2 = np.asarray(inputs["L0_ffn_W2"])
    b2 = np.asarray(inputs["L0_ffn_b2"])

    cst = host_constants()
    shared = {
        "wqT": np.ascontiguousarray(wq.T), "wkT": np.ascontiguousarray(wk.T),
        "wvT": np.ascontiguousarray(wv.T), "woT": np.ascontiguousarray(wo.T),
        "W1": np.ascontiguousarray(W1), "W2": np.ascontiguousarray(W2),
        "bq": bq.reshape(1, -1), "bk": bk.reshape(1, -1), "bv": bv.reshape(1, -1),
        "bo": bo.reshape(1, -1), "b1": b1.reshape(1, -1), "b2": b2.reshape(1, -1),
        "relb": np.ascontiguousarray(relb),
        "relb0": np.ascontiguousarray(np.tile(relb[0:1, :], (32, 1))),
        "relbT": np.ascontiguousarray(relb.T),
        "OH": cst["OH"], "OHP": cst["OHP"], "ident": cst["ident"], "bd": cst["bd"],
    }
    in_maps = []
    for i in range(8):
        b_, h_ = i // 2, i % 2
        m = dict(shared)
        m["x"] = np.ascontiguousarray(x[b_, h_ * T:(h_ + 1) * T, :])
        m["par2"] = np.array([[h_, 1 - h_]], np.int32)
        m["parf"] = np.full((32, 1), float(h_), np.float32)
        in_maps.append(m)

    nc = build("full")
    mybir.codegen_inst_isa_subclasses(nc)
    res = run_bass_kernel_spmd(nc, in_maps, core_ids=list(range(8)))
    out = np.empty((4, 4096, 1024), np.float32)
    for i in range(8):
        b_, h_ = i // 2, i % 2
        out[b_, h_ * T:(h_ + 1) * T, :] = res.results[i]["y"]
    return out


if __name__ == "__main__":
    nc = build("full")
    print("instructions:", nc.n_instructions())
